# version 10
# baseline (speedup 1.0000x reference)
import numpy as np
import concourse.bass as bass
import concourse.mybir as mybir
from concourse.bass_utils import run_bass_kernel_spmd

F32 = mybir.dt.float32
F16 = mybir.dt.float16
AF = mybir.ActivationFunctionType
ALU = mybir.AluOpType
AX = mybir.AxisListType

D = 2048
NCH = 16
HEADS = 16
NGRP = 4
SEQ = 4096
CH = 512
PA, PB = 2048, 4096
TA, TB = PA // 128, PB // 128
HFF = 5120
NHH = 40
NQUARTER = 4
HH_PER_Q = 10
SCALE = float(1.0 / np.sqrt(np.float32(128.0)))
NEG_BIG = -1.0e30
EPS = 1e-5

STAT_CHUNK = 512
N_STAT_FC = (NHH * D * 128) // (128 * STAT_CHUNK)
N_STAT_PR = (NCH * HFF * 128) // (128 * STAT_CHUNK)
N_STAT = N_STAT_FC + N_STAT_PR

MB_ONESC = 0
MB_NEGM = 1
MB_THRF = 2
MB_NTHRF = 3
MB_THRP = 4
MB_NTHRP = 5
MB_CC = 6
MB_NEGM2 = 7
MB_RED0 = 8
MB_KMAX = 10

SR_KN = 0
SR_RS = 1
SR_MEAN = 2
SR_RSTD = 3
SR_TMP = 4
SR_TMP2 = 5
SR_M11 = 6
SR_NEGMROW = 7


class P:

    def __init__(self, nc, sems, eng, handles):
        self.nc = nc
        self.sems = sems
        self.eng = eng
        self.h = handles
        self.c = {s: 0 for s in sems}

    def do(self, eng, fn, incs=()):
        if self.eng == eng:
            inst = fn(self.h[eng])
            for s, n in incs:
                inst.then_inc(self.sems[s], n)
        for s, n in incs:
            self.c[s] += n

    def wait(self, eng, sem, val):
        assert val <= self.c[sem], \
            f"unsatisfiable wait {eng} {sem}>={val} (planned {self.c[sem]})"
        if val > 0 and self.eng == eng:
            self.h[eng].wait_ge(self.sems[sem], val)

    def barrier(self):
        snap = {e: self.c[e] for e in ("pe", "act", "dve")}
        for e in ("pe", "act", "dve", "sp"):
            for o in ("pe", "act", "dve"):
                if o != e:
                    self.wait(e, o, snap[o])


def srow(T, r, n=512):
    return T["stats"][r:r + 1, 0:n]


def build_program():
    nc = bass.Bass("TRN2", target_bir_lowering=False)

    dr = {}
    dr["xkvA"] = nc.dram_tensor("xkvA", [D, PA], F32, kind="ExternalInput").ap()
    dr["xkvB"] = nc.dram_tensor("xkvB", [D, PB], F32, kind="ExternalInput").ap()
    dr["validA"] = nc.dram_tensor("validA", [128, TA], F32, kind="ExternalInput").ap()
    dr["validB"] = nc.dram_tensor("validB", [128, TB], F32, kind="ExternalInput").ap()
    dr["tri"] = nc.dram_tensor("tri", [128, 128], F32, kind="ExternalInput").ap()
    dr["wq"] = nc.dram_tensor("wq", [HEADS, D, 128], F32, kind="ExternalInput").ap()
    dr["wk"] = nc.dram_tensor("wk", [NGRP, D, 128], F32, kind="ExternalInput").ap()
    dr["wv"] = nc.dram_tensor("wv", [D, 512], F32, kind="ExternalInput").ap()
    dr["wo"] = nc.dram_tensor("wo", [NCH, D, 128], F32, kind="ExternalInput").ap()
    dr["bo"] = nc.dram_tensor("bo", [128, NCH], F32, kind="ExternalInput").ap()
    dr["wfc"] = nc.dram_tensor("wfc", [NHH, D, 128], F32, kind="ExternalInput").ap()
    dr["wpr"] = nc.dram_tensor("wpr", [NCH, HFF, 128], F32, kind="ExternalInput").ap()
    dr["y2"] = nc.dram_tensor("y2", [D, 2 * CH], F32, kind="ExternalOutput").ap()

    from contextlib import ExitStack
    with ExitStack() as _es:
        def A(cm):
            return _es.enter_context(cm)
        aKY = A(nc.sbuf_tensor("aKY", [128, 8192], F32))
        aV = A(nc.sbuf_tensor("aV", [128, 8192], F32))
        aXQ = A(nc.sbuf_tensor("aXQ", [128, 8192], F32))
        aCTX = A(nc.sbuf_tensor("aCTX", [128, 8192], F32))
        aSTR = A(nc.sbuf_tensor("aSTR", [128, 12288], F32))
        tri_sb = A(nc.sbuf_tensor("tri_sb", [128, 128], F32))
        valid_sb = A(nc.sbuf_tensor("valid_sb", [128, TA + TB], F32))
        bo_sb = A(nc.sbuf_tensor("bo_sb", [128, NCH], F32))
        stats_sb = A(nc.sbuf_tensor("stats_sb", [128, 512], F32))
        scr = A(nc.sbuf_tensor("scr", [128, 512], F32))
        rsb = A(nc.sbuf_tensor("rsb", [128, 512], F32))
        mb = A(nc.sbuf_tensor("mb", [128, 64], F32))
        ones1 = A(nc.sbuf_tensor("ones1", [1, 128], F32))
        rowA = A(nc.sbuf_tensor("rowA", [1, 512], F32))
        rowB = A(nc.sbuf_tensor("rowB", [1, 512], F32))
        rowC = A(nc.sbuf_tensor("rowC", [1, 512], F32))
        rowD = A(nc.sbuf_tensor("rowD", [1, 64], F32))
        rowE = A(nc.sbuf_tensor("rowE", [1, 512], F32))
        ones16 = A(nc.sbuf_tensor("ones16", [128, 1], F16))
        ctmp = A(nc.sbuf_tensor("ctmp", [128, 512], F32))
        validsc = A(nc.sbuf_tensor("validsc", [128, TA + TB], F32))
        psA = A(nc.psum_tensor("psA", [128, 512], F32))
        psB = A(nc.psum_tensor("psB", [128, 512], F32))
        psC = A(nc.psum_tensor("psC", [128, 512], F32))
        psD = A(nc.psum_tensor("psD", [128, 512], F32))
        psE = A(nc.psum_tensor("psE", [128, 512], F32))
        psF = A(nc.psum_tensor("psF", [128, 512], F32))
        psS1 = A(nc.psum_tensor("psS1", [1, 512], F32))
        psS2 = A(nc.psum_tensor("psS2", [1, 512], F32))
        s_ld = A(nc.semaphore("ld"))
        s_gd = A(nc.semaphore("gd"))
        s_so = A(nc.semaphore("so"))
        s_pe = A(nc.semaphore("pe"))
        s_act = A(nc.semaphore("act"))
        s_dve = A(nc.semaphore("dve"))
        block = A(nc.Block())
        sems = {"ld": s_ld, "gd": s_gd, "so": s_so,
                "pe": s_pe, "act": s_act, "dve": s_dve}
        T = {
            "aKY": aKY, "aV": aV, "aXQ": aXQ, "aCTX": aCTX, "aSTR": aSTR,
            "tri": tri_sb, "valid": valid_sb, "bo": bo_sb,
            "stats": stats_sb, "scr": scr, "rsb": rsb, "mb": mb, "ones1": ones1,
            "rowA": rowA, "rowB": rowB, "rowC": rowC, "rowD": rowD, "rowE": rowE,
            "ones16": ones16, "ctmp": ctmp, "validsc": validsc,
            "psA": psA, "psB": psB, "psC": psC, "psD": psD,
            "psE": psE, "psF": psF, "psS1": psS1, "psS2": psS2,
        }

        @block.tensor
        def _(tensor):
            program(P(nc, sems, "pe", {"pe": tensor}), T, dr)

        @block.scalar
        def _(scalar):
            program(P(nc, sems, "act", {"act": scalar}), T, dr)

        @block.vector
        def _(vector):
            program(P(nc, sems, "dve", {"dve": vector}), T, dr)

        @block.sync
        def _(sync):
            program(P(nc, sems, "sp", {"sp": sync}), T, dr)

        @block.gpsimd
        def _(gpsimd):
            program(P(nc, sems, "gp", {"gp": gpsimd}), T, dr)

    return nc



def program(P_, T, dr):
    st = {}
    prologue(P_, T, dr, st)
    stats_plan(P_, T, dr, st)
    macro(P_, T, dr, st, "A")
    macro(P_, T, dr, st, "B")
    P_.wait("sp", "so", P_.c["so"])


def prologue(P_, T, dr, st):
    P_.do("sp", lambda e: e.dma_start(out=T["tri"][:], in_=dr["tri"][:]), [("ld", 16)])
    P_.do("sp", lambda e: e.dma_start(out=T["valid"][:, 0:TA], in_=dr["validA"][:]), [("ld", 16)])
    P_.do("sp", lambda e: e.dma_start(out=T["valid"][:, TA:TA + TB], in_=dr["validB"][:]), [("ld", 16)])
    P_.do("sp", lambda e: e.dma_start(out=T["bo"][:], in_=dr["bo"][:]), [("ld", 16)])
    st["consts_ld"] = P_.c["ld"]
    P_.do("dve", lambda e: e.memset(T["ones1"][:], 1.0))
    P_.do("dve", lambda e: e.memset(T["mb"][:, MB_ONESC:MB_ONESC + 1], 1.0))
    P_.do("dve", lambda e: e.memset(T["rowD"][0:1, 48:49], EPS))
    P_.do("dve", lambda e: e.memset(T["ones16"][:], 1.0), [("dve", 1)])
    P_.wait("dve", "ld", st["consts_ld"])
    P_.do("dve", lambda e: e.tensor_scalar(T["validsc"][:], T["valid"][:], 16384.0,
                                           None, ALU.mult), [("dve", 1)])
    st["consts_dve"] = P_.c["dve"]
    P_.wait("pe", "dve", st["consts_dve"])
    P_.wait("act", "dve", st["consts_dve"])
    P_.wait("dve", "ld", st["consts_ld"])
    P_.wait("pe", "ld", st["consts_ld"])
    P_.wait("act", "ld", st["consts_ld"])


def stats_plan(P_, T, dr, st):
    ring = [T["aV"][:, 4096 + 512 * j: 4096 + 512 * (j + 1)] for j in range(4)]
    acc = T["aV"][:, 6400:6400 + N_STAT]

    def src_ap(j):
        if j < N_STAT_FC:
            a, q = divmod(j, 4)
            return dr["wfc"][a].rearrange("(q p) m -> p q m", p=128)[:, 4 * q:4 * (q + 1), :]
        j -= N_STAT_FC
        a, q = divmod(j, 10)
        return dr["wpr"][a].rearrange("(q p) m -> p q m", p=128)[:, 4 * q:4 * (q + 1), :]

    st["stats_dma_n"] = 0
    st["stats_abs_n"] = 0
    st["stats_absdone_act"] = []
    st["stats_ready_gd"] = []

    def emit_dma(j):
        if j >= 4:
            P_.wait("gp", "act", st["stats_absdone_act"][j - 4])
        P_.do("gp", lambda e, j=j: e.dma_start(
            out=ring[j % 4].rearrange("p (q m) -> p q m", m=128),
            in_=src_ap(j)), [("gd", 16)])
        st["stats_ready_gd"].append(P_.c["gd"])

    def pump(n):
        target = min(st["stats_abs_n"] + n, N_STAT)
        while st["stats_abs_n"] < target:
            while st["stats_dma_n"] < min(st["stats_abs_n"] + 4, N_STAT):
                emit_dma(st["stats_dma_n"])
                st["stats_dma_n"] += 1
            j = st["stats_abs_n"]
            P_.wait("act", "gd", st["stats_ready_gd"][j])
            P_.do("act", lambda e, j=j: e.activation(
                ring[j % 4][:], ring[j % 4][:], AF.Abs,
                accum_out=acc[:, j:j + 1]), [("act", 1)])
            st["stats_absdone_act"].append(P_.c["act"])
            st["stats_abs_n"] = j + 1

    st["stats_pump"] = pump
    st["stats_acc"] = acc


def stats_finalize(P_, T, dr, st):
    if st.get("stats_final_done"):
        return
    st["stats_pump"](N_STAT)
    acc = st["stats_acc"]
    P_.wait("dve", "act", P_.c["act"])
    P_.do("dve", lambda e: e.reduce_sum(T["mb"][:, MB_RED0:MB_RED0 + 1],
                                        acc[:, 0:N_STAT_FC], axis=AX.X))
    P_.do("dve", lambda e: e.reduce_sum(T["mb"][:, MB_RED0 + 1:MB_RED0 + 2],
                                        acc[:, N_STAT_FC:N_STAT], axis=AX.X), [("dve", 1)])
    P_.wait("pe", "dve", P_.c["dve"])
    P_.do("pe", lambda e: e.matmul(T["psS1"][0:1, 0:1], T["mb"][:, MB_RED0:MB_RED0 + 1],
                                   T["mb"][:, MB_ONESC:MB_ONESC + 1], start=True, stop=True))
    P_.do("pe", lambda e: e.matmul(T["psS2"][0:1, 0:1], T["mb"][:, MB_RED0 + 1:MB_RED0 + 2],
                                   T["mb"][:, MB_ONESC:MB_ONESC + 1], start=True, stop=True),
          [("pe", 1)])
    P_.wait("dve", "pe", P_.c["pe"])
    r = T["rowD"][0:1, 0:16]
    NFC = float(NHH * D * 128)
    NPR = float(NCH * HFF * 128)
    P_.do("dve", lambda e: e.tensor_scalar(r[0:1, 0:1], T["psS1"][0:1, 0:1], 1.0 / NFC, 1e-5,
                                           ALU.mult, ALU.max))
    P_.do("dve", lambda e: e.tensor_scalar(r[0:1, 1:2], T["psS2"][0:1, 0:1], 1.0 / NPR, 1e-5,
                                           ALU.mult, ALU.max))
    P_.do("dve", lambda e: e.tensor_scalar(r[0:1, 2:3], r[0:1, 0:1], 0.7, None, ALU.mult))
    P_.do("dve", lambda e: e.tensor_scalar(r[0:1, 3:4], r[0:1, 1:2], 0.7, None, ALU.mult))
    P_.do("dve", lambda e: e.tensor_scalar(r[0:1, 4:5], r[0:1, 2:3], -1.0, None, ALU.mult))
    P_.do("dve", lambda e: e.tensor_scalar(r[0:1, 8:9], r[0:1, 0:1], 16.0, None, ALU.mult))
    P_.do("dve", lambda e: e.tensor_scalar(r[0:1, 5:6], r[0:1, 3:4], -1.0, None, ALU.mult))
    P_.do("dve", lambda e: e.tensor_scalar(r[0:1, 7:8], r[0:1, 1:2], r[0:1, 8:9],
                                           r[0:1, 8:9], ALU.mult, ALU.mult), [("dve", 1)])
    P_.wait("pe", "dve", P_.c["dve"])
    pairs = [(2, MB_THRF), (4, MB_NTHRF), (3, MB_THRP), (5, MB_NTHRP), (7, MB_CC)]
    for k, (src, dstc) in enumerate(pairs):
        P_.do("pe", lambda e, src=src, dstc=dstc: e.matmul(
            T["psE"][:, dstc:dstc + 1], T["ones1"][:], r[0:1, src:src + 1],
            start=True, stop=True), [("pe", 1)] if k == len(pairs) - 1 else ())
    P_.wait("dve", "pe", P_.c["pe"])
    P_.do("dve", lambda e: e.tensor_copy(T["mb"][:, MB_THRF:MB_CC + 1],
                                         T["psE"][:, MB_THRF:MB_CC + 1]), [("dve", 1)])
    st["ffn_consts_dve"] = P_.c["dve"]
    st["stats_final_done"] = True



def macro(P_, T, dr, st, which):
    TT = TA if which == "A" else TB
    xkv = dr["xkvA"] if which == "A" else dr["xkvB"]
    voff = 0 if which == "A" else TA
    out_col = 0 if which == "A" else CH

    for gp_i in (0, 1):
        kv_pass(P_, T, dr, st, which, gp_i, TT, xkv)
        if gp_i == 0:
            split_xq(P_, T, dr, st, which)
        attn(P_, T, dr, st, which, gp_i, TT, voff)
    wo_ln1(P_, T, dr, st, which)
    stats_finalize(P_, T, dr, st)
    ffn(P_, T, dr, st, which)
    ln2_out(P_, T, dr, st, which, out_col)


def k16_tile(T, gl, t128):
    t256, sub = divmod(t128, 2)
    base = 8192 * gl + 512 * t256
    k16 = T["aKY"][:].bitcast(F16)
    hi = k16[:, base + 128 * sub: base + 128 * sub + 128]
    lo = k16[:, base + 256 + 128 * sub: base + 256 + 128 * sub + 128]
    return hi, lo


def v16_tile(T, t128):
    v16 = T["aV"][:].bitcast(F16)
    base = 512 * t128
    return v16[:, base:base + 256], v16[:, base + 256:base + 512]


def xq16(T):
    x16 = T["aXQ"][:].bitcast(F16).rearrange("p (c n) -> p c n", n=1024)
    return x16[:, :, 0:512], x16[:, :, 512:1024]


def ctx16(T):
    c16 = T["aCTX"][:].bitcast(F16).rearrange("p (c n) -> p c n", n=1024)
    return c16[:, :, 0:512], c16[:, :, 512:1024]


def split_pair(P_, T, st, src32, hi, lo, tag):
    P_.do("act", lambda e: e.activation(hi, src32, AF.Copy), [("act", 1)])
    st[("split_hi", tag)] = P_.c["act"]
    P_.wait("dve", "act", P_.c["act"])
    P_.do("dve", lambda e: e.scalar_tensor_tensor(lo, src32, 1.0, hi,
                                                  ALU.mult, ALU.subtract), [("dve", 1)])
    st[("split_lo", tag)] = P_.c["dve"]


def kv_pass(P_, T, dr, st, which, gp_i, TT, xkv):
    P_.barrier()
    T256 = TT // 2
    voff = 0 if which == "A" else TA
    str16 = T["aSTR"][:].bitcast(F16)
    wkPh = str16[:, 0:8192].rearrange("p (c n) -> p c n", n=512)[:, :, 0:256]
    wkPl = str16[:, 0:8192].rearrange("p (c n) -> p c n", n=512)[:, :, 256:512]
    wvPh = str16[:, 8192:16384].rearrange("p (c n) -> p c n", n=512)[:, :, 0:256]
    wvPl = str16[:, 8192:16384].rearrange("p (c n) -> p c n", n=512)[:, :, 256:512]
    xPh = str16[:, 16384:24576].rearrange("p (c n) -> p c n", n=512)[:, :, 0:256]
    xPl = str16[:, 16384:24576].rearrange("p (c n) -> p c n", n=512)[:, :, 256:512]
    wdst = T["aCTX"][:, 4096:8192]
    wdst3 = wdst.rearrange("p (c n) -> p c n", n=256)

    for gl in range(2):
        g = 2 * gp_i + gl
        P_.do("sp", lambda e, g=g, gl=gl: e.dma_start(
            out=wdst3[:, :, 128 * gl:128 * (gl + 1)],
            in_=dr["wk"][g].rearrange("(c p) m -> p c m", p=128)), [("ld", 16)])
    P_.wait("act", "ld", P_.c["ld"])
    split_pair(P_, T, st, wdst3, wkPh, wkPl, ("wk", which, gp_i))
    P_.wait("sp", "act", P_.c["act"])
    P_.wait("sp", "dve", P_.c["dve"])
    P_.do("sp", lambda e, gp_i=gp_i: e.dma_start(
        out=wdst3,
        in_=dr["wv"][:, 256 * gp_i:256 * (gp_i + 1)].rearrange("(c p) m -> p c m", p=128)),
        [("ld", 16)])
    P_.wait("act", "ld", P_.c["ld"])
    split_pair(P_, T, st, wdst3, wvPh, wvPl, ("wv", which, gp_i))

    P_.do("dve", lambda e: e.memset(T["mb"][0:1, MB_KMAX:MB_KMAX + 2], 0.0), [("dve", 1)])

    for t in range(T256):
        P_.wait("sp", "act", st.get(("split_hi", ("x", which, gp_i, t - 1)), 0))
        P_.wait("sp", "dve", st.get(("split_lo", ("x", which, gp_i, t - 1)), 0))
        if t == 0:
            P_.wait("sp", "act", st[("split_hi", ("wv", which, gp_i))])
            P_.wait("sp", "dve", st[("split_lo", ("wv", which, gp_i))])
        P_.do("sp", lambda e, t=t: e.dma_start(
            out=wdst3,
            in_=xkv[:, 256 * t:256 * (t + 1)].rearrange("(c p) n -> p c n", p=128)),
            [("ld", 16)])
        x_ld = P_.c["ld"]
        if t >= T256 - 2 and gp_i == 0:
            dcol = 256 * (t - (T256 - 2))
            P_.do("sp", lambda e, t=t, dcol=dcol: e.dma_start(
                out=T["aXQ"].rearrange("p (c n) -> p c n", n=512)[:, :, dcol:dcol + 256],
                in_=xkv[:, 256 * t:256 * (t + 1)].rearrange("(c p) n -> p c n", p=128)),
                [("ld", 16)])
            x_ld = P_.c["ld"]
        P_.wait("act", "ld", x_ld)
        P_.wait("act", "pe", st.get(("kv_mm_pe", which, gp_i, t - 1), 0))
        P_.wait("dve", "pe", st.get(("kv_mm_pe", which, gp_i, t - 1), 0))
        split_pair(P_, T, st, wdst3, xPh, xPl, ("x", which, gp_i, t))
        xs_hi = st[("split_hi", ("x", which, gp_i, t))]
        xs_lo = st[("split_lo", ("x", which, gp_i, t))]

        kps = [T["psA"], T["psB"]]
        for gl in range(2):
            idx = 2 * t + gl
            bank = kps[idx % 2]
            P_.wait("pe", "act", max(st.get(("kv_evac", which, gp_i, idx - 2), 0), xs_hi))
            P_.wait("pe", "dve", max(st.get(("kv_evac_lo", which, gp_i, idx - 2), 0), xs_lo))
            first = True
            for (wv_, xv_) in ((wkPh, xPh), (wkPh, xPl), (wkPl, xPh)):
                for c in range(NCH):
                    last = (not first) and False
                    P_.do("pe", lambda e, c=c, gl=gl, bank=bank, wv_=wv_, xv_=xv_: e.matmul(
                        bank[:, 0:256], wv_[:, c, 128 * gl:128 * (gl + 1)], xv_[:, c, :],
                        start=first, stop=False),
                        ())
                    first = False
            P_.do("pe", lambda e, gl=gl, bank=bank: e.matmul(
                bank[:, 0:256], wkPl[:, NCH - 1, 128 * gl:128 * (gl + 1)],
                xPl[:, NCH - 1, :], start=False, stop=True), [("pe", 1)])
            kmm = P_.c["pe"]
            t128a, t128b = 2 * t, 2 * t + 1
            t256 = t
            base = 8192 * gl + 512 * t256
            k16 = T["aKY"][:].bitcast(F16)
            khi = k16[:, base:base + 256]
            klo = k16[:, base + 256:base + 512]
            P_.wait("act", "pe", kmm)
            P_.do("act", lambda e, bank=bank, khi=khi: e.activation(khi, bank[:, 0:256],
                                                                    AF.Copy))
            P_.do("act", lambda e, bank=bank, gl=gl: e.activation(
                T["scr"][:, 256 * gl:256 * (gl + 1)], bank[:, 0:256], AF.Square),
                [("act", 1)])
            st[("kv_evac", which, gp_i, idx)] = P_.c["act"]
            P_.wait("dve", "act", P_.c["act"])
            P_.do("dve", lambda e, bank=bank, khi=khi, klo=klo: e.scalar_tensor_tensor(
                klo, bank[:, 0:256], 1.0, khi, ALU.mult, ALU.subtract), [("dve", 1)])
            st[("kv_evac_lo", which, gp_i, idx)] = P_.c["dve"]

        for sub in range(2):
            t128 = 2 * t + sub
            vbank = [T["psC"], T["psD"]][t128 % 2]
            P_.wait("pe", "dve", st.get(("v_evac_lo", which, gp_i, t128 - 2), 0))
            P_.wait("pe", "act", st.get(("v_evac", which, gp_i, t128 - 2), 0))
            passes = [(xPh, wvPh), (xPl, wvPh), (xPh, wvPl)]
            for pi, (xv_, wv_) in enumerate(passes):
                for c in range(NCH):
                    last = (pi == 2 and c == NCH - 1)
                    P_.do("pe", lambda e, c=c, sub=sub, vbank=vbank, xv_=xv_, wv_=wv_,
                          pi=pi, last=last: e.matmul(
                              vbank[:, 0:256],
                              xv_[:, c, 128 * sub:128 * (sub + 1)], wv_[:, c, :],
                              start=(pi == 0 and c == 0), stop=last),
                        [("pe", 1)] if last else ())
            st[("kv_mm_pe", which, gp_i, t)] = P_.c["pe"]
            vhi, vlo = v16_tile(T, t128)
            P_.wait("act", "pe", P_.c["pe"])
            P_.do("act", lambda e, vbank=vbank, vhi=vhi, t128=t128, voff=voff: e.activation(
                vhi, vbank[:, 0:256], AF.Copy,
                scale=T["valid"][:, voff + t128:voff + t128 + 1]), [("act", 1)])
            st[("v_evac", which, gp_i, t128)] = P_.c["act"]
            P_.wait("dve", "act", P_.c["act"])
            P_.do("dve", lambda e, vbank=vbank, vhi=vhi, vlo=vlo, t128=t128, voff=voff:
                  e.scalar_tensor_tensor(vlo, vbank[:, 0:256],
                                         T["valid"][:, voff + t128:voff + t128 + 1],
                                         vhi, ALU.mult, ALU.subtract), [("dve", 1)])
            st[("v_evac_lo", which, gp_i, t128)] = P_.c["dve"]

        for gl in range(2):
            idx = 2 * t + gl
            sbank = [T["psS1"], T["psS2"]][gl]
            P_.wait("pe", "act", st[("kv_evac", which, gp_i, idx)])
            P_.wait("pe", "dve", st.get(("knorm_dve", which, gp_i, idx - 2), 0))
            P_.do("pe", lambda e, gl=gl, sbank=sbank: e.matmul(
                sbank[0:1, 0:256], T["mb"][:, MB_ONESC:MB_ONESC + 1],
                T["scr"][:, 256 * gl:256 * (gl + 1)], start=True, stop=True), [("pe", 1)])
            P_.wait("dve", "pe", P_.c["pe"])
            P_.do("dve", lambda e, sbank=sbank: e.reduce_max(
                srow(T, SR_KN, 1), sbank[0:1, 0:256], axis=AX.X))
            P_.do("dve", lambda e: e.tensor_copy(T["mb"][0:1, 36:40], T["mb"][0:1, 40:44]))
            P_.do("dve", lambda e, gl=gl: e.tensor_tensor(
                T["mb"][0:1, MB_KMAX + gl:MB_KMAX + gl + 1],
                T["mb"][0:1, MB_KMAX + gl:MB_KMAX + gl + 1],
                srow(T, SR_KN, 1), ALU.max), [("dve", 1)])
            st[("knorm_dve", which, gp_i, idx)] = P_.c["dve"]

        if which == "A":
            st["stats_pump"](10)

    st[("kmax_dve", which, gp_i)] = P_.c["dve"]


def split_xq(P_, T, dr, st, which):
    P_.barrier()
    xqv = T["aXQ"].rearrange("p (c n) -> p c n", n=512)
    xh, xl = xq16(T)
    for c in range(NCH):
        P_.do("dve", lambda e, c=c: e.tensor_copy(T["ctmp"][:], xqv[:, c, :]), [("dve", 1)])
        P_.wait("act", "dve", P_.c["dve"])
        P_.do("act", lambda e, c=c: e.activation(xh[:, c, :], T["ctmp"][:], AF.Copy),
              [("act", 1)])
        P_.wait("dve", "act", P_.c["act"])
        P_.do("dve", lambda e, c=c: e.scalar_tensor_tensor(
            xl[:, c, :], T["ctmp"][:], 1.0, xh[:, c, :], ALU.mult, ALU.subtract),
            [("dve", 1)])
    st[("xqsplit_dve", which)] = P_.c["dve"]
    st[("xqsplit_act", which)] = P_.c["act"]


def attn(P_, T, dr, st, which, gp_i, TT, voff):
    P_.barrier()
    str16 = T["aSTR"][:].bitcast(F16)
    wqD = T["aSTR"][:, 0:2048]
    wqD3 = wqD.rearrange("p (c n) -> p c n", n=128)
    wqP_sl = []
    for k in range(2):
        base = T["aSTR"][:, 2048 * (k + 1):2048 * (k + 2)].bitcast(F16)
        v = base.rearrange("p (c n) -> p c n", n=256)
        wqP_sl.append((v[:, :, 0:128], v[:, :, 128:256]))
    qh_sl = [(str16[:, 12288:12800], str16[:, 12800:13312]),
             (str16[:, 13312:13824], str16[:, 13824:14336])]
    pt32 = [T["aSTR"][:, 7168 + 512 * k:7680 + 512 * k] for k in range(4)]
    ptP = [(str16[:, 18432 + 1024 * k:18944 + 1024 * k],
            str16[:, 18944 + 1024 * k:19456 + 1024 * k]) for k in range(4)]
    xh, xl = xq16(T)
    ch_, cl_ = ctx16(T)

    def wq_fetch(j):
        if j >= 8:
            return
        H = 8 * gp_i + j
        P_.wait("sp", "act", st.get(("split_hi", ("wq", which, gp_i, j - 1)), 0))
        P_.wait("sp", "dve", st.get(("split_lo", ("wq", which, gp_i, j - 1)), 0))
        P_.do("sp", lambda e, H=H: e.dma_start(
            out=wqD3, in_=dr["wq"][H].rearrange("(c p) m -> p c m", p=128)), [("ld", 16)])
        P_.wait("act", "ld", P_.c["ld"])
        P_.wait("act", "pe", st.get(("qproj_pe", which, 8 * gp_i + j - 1), 0))
        P_.wait("dve", "pe", st.get(("qproj_pe", which, 8 * gp_i + j - 1), 0))
        split_pair(P_, T, st, wqD3, wqP_sl[j % 2][0], wqP_sl[j % 2][1],
                   ("wq", which, gp_i, j))

    def head_front(j):
        if j >= 8:
            return
        H = 8 * gp_i + j
        gl = (H // 4) - 2 * gp_i
        hb = j % 2
        qbank = [T["psE"], T["psF"]][hb]
        wqPh, wqPl = wqP_sl[j % 2]
        negm_col = [MB_NEGM, MB_NEGM2][hb]
        qhh, qhl = qh_sl[hb]

        P_.wait("pe", "act", st[("split_hi", ("wq", which, gp_i, j))])
        P_.wait("pe", "dve", st[("split_lo", ("wq", which, gp_i, j))])
        if j >= 2:
            P_.wait("pe", "act", st[("q_evac_act", which, H - 2)])
            P_.wait("pe", "dve", st[("ctx_dve", which, H - 2)])
        for pi, (wv_, xv_) in enumerate([(wqPh, xh), (wqPh, xl), (wqPl, xh)]):
            for c in range(NCH):
                last = (pi == 2 and c == NCH - 1)
                P_.do("pe", lambda e, c=c, qbank=qbank, wv_=wv_, xv_=xv_, pi=pi,
                      last=last: e.matmul(
                    qbank[:], wv_[:, c, :], xv_[:, c, :],
                    start=(pi == 0 and c == 0), stop=last), [("pe", 1)] if last else ())
        st[("qproj_pe", which, H)] = P_.c["pe"]

        P_.wait("act", "pe", P_.c["pe"])
        P_.do("act", lambda e, qhh=qhh, qbank=qbank: e.activation(qhh, qbank[:], AF.Copy))
        P_.do("act", lambda e, qbank=qbank: e.activation(T["scr"][:], qbank[:], AF.Square),
              [("act", 1)])
        st[("q_evac_act", which, H)] = P_.c["act"]
        P_.wait("dve", "act", P_.c["act"])
        P_.do("dve", lambda e, qhh=qhh, qhl=qhl, qbank=qbank: e.scalar_tensor_tensor(
            qhl, qbank[:], 1.0, qhh, ALU.mult, ALU.subtract), [("dve", 1)])
        st[("q_lo_dve", which, H)] = P_.c["dve"]
        P_.wait("pe", "act", st[("q_evac_act", which, H)])
        P_.wait("pe", "dve", P_.c["dve"])
        P_.do("pe", lambda e, qbank=qbank: e.matmul(
            qbank[0:1, 0:512], T["mb"][:, MB_ONESC:MB_ONESC + 1], T["scr"][:],
            start=True, stop=True), [("pe", 1)])
        P_.wait("dve", "pe", P_.c["pe"])
        P_.do("dve", lambda e, qbank=qbank: e.reduce_max(srow(T, SR_KN, 1),
                                                         qbank[0:1, 0:512], axis=AX.X))
        P_.do("dve", lambda e: e.tensor_copy(T["mb"][0:1, 36:40], T["mb"][0:1, 40:44]))
        P_.do("dve", lambda e, gl=gl: e.tensor_tensor(
            T["rowD"][0:1, 34:35], srow(T, SR_KN, 1),
            T["mb"][0:1, MB_KMAX + gl:MB_KMAX + gl + 1], ALU.mult), [("dve", 1)])
        P_.wait("act", "dve", P_.c["dve"])
        P_.do("act", lambda e: e.activation(T["rowD"][0:1, 32:33], T["rowD"][0:1, 34:35],
                                            AF.Sqrt, scale=1.0 / 128.0), [("act", 1)])
        P_.wait("dve", "act", P_.c["act"])
        P_.do("dve", lambda e: e.tensor_scalar(T["rowD"][0:1, 32:33], T["rowD"][0:1, 32:33],
                                               -1.0, None, ALU.mult), [("dve", 1)])
        P_.wait("pe", "dve", P_.c["dve"])
        P_.do("pe", lambda e, qbank=qbank: e.matmul(
            qbank[:, 0:1], T["ones1"][:], T["rowD"][0:1, 32:33],
            start=True, stop=True), [("pe", 1)])
        P_.wait("dve", "pe", P_.c["pe"])
        P_.do("dve", lambda e, qbank=qbank, negm_col=negm_col: e.tensor_copy(
            T["mb"][:, negm_col:negm_col + 1], qbank[:, 0:1]), [("dve", 1)])
        st[("negm_dve", which, H)] = P_.c["dve"]

    wq_fetch(0)
    head_front(0)
    for j in range(8):
        H = 8 * gp_i + j
        gl = (H // 4) - 2 * gp_i
        hb = j % 2
        qbank = [T["psE"], T["psF"]][hb]
        ctxps = [T["psC"], T["psD"]][hb]
        sps = [T["psS1"], T["psS2"]][hb]
        qhh, qhl = qh_sl[hb]
        negm_col = [MB_NEGM, MB_NEGM2][hb]
        negm_dve = st[("negm_dve", which, H)]
        wq_fetch(j + 1)

        stps = [T["psA"], T["psB"]]

        def emit_S(t):
            diag_j = t - (TT - 4)
            qlo = 128 * diag_j if diag_j >= 0 else 0
            bank = stps[t % 2]
            P_.wait("pe", "act", st.get(("exp_act", which, H, t - 2), 0))
            khi, klo = k16_tile(T, gl, t)
            first = True
            for (kv_, qv_) in ((khi, qhh), (khi, qhl), (klo, qhh)):
                last = (kv_ is klo)
                P_.do("pe", lambda e, t=t, qlo=qlo, bank=bank, kv_=kv_, qv_=qv_,
                      first=first, last=last: e.matmul(
                          bank[:, qlo:512], kv_, qv_[:, qlo:512],
                          start=first, stop=last), [("pe", 1)] if last else ())
                first = False
            st[("st_pe", which, H, t)] = P_.c["pe"]

        def qlo_of(t):
            diag_j = t - (TT - 4)
            return 128 * diag_j if diag_j >= 0 else 0

        def emit_lo(t):
            qlo = qlo_of(t)
            w = 512 - qlo
            pth, ptl = ptP[t % 4]
            P_.wait("dve", "act", st[("pth_act", which, H, t)])
            P_.do("dve", lambda e, t=t, w=w, pth=pth, ptl=ptl, voff=voff:
                  e.scalar_tensor_tensor(ptl[:, 0:w], pt32[t % 4][:, 0:w],
                                         T["validsc"][:, voff + t:voff + t + 1],
                                         pth[:, 0:w], ALU.mult, ALU.subtract),
                  [("dve", 1)])
            st[("ptl_dve", which, H, t)] = P_.c["dve"]
            P_.wait("pe", "dve", P_.c["dve"])
            vhi, vlo = v16_tile(T, t)
            P_.do("pe", lambda e, t=t, qlo=qlo, w=w, sps=sps, ptl=ptl: e.matmul(
                sps[0:1, qlo:512], T["ones16"][:], ptl[:, 0:w],
                start=False, stop=(t == TT - 1)))
            P_.do("pe", lambda e, t=t, qlo=qlo, w=w, gl=gl, ctxps=ctxps, vhi=vhi,
                  ptl=ptl: e.matmul(
                      ctxps[:, qlo:512], vhi[:, 128 * gl:128 * (gl + 1)], ptl[:, 0:w],
                      start=False, stop=(t == TT - 1)), [("pe", 1)])
            st[("lo_pe", which, H, t)] = P_.c["pe"]

        emit_S(0)
        for t in range(TT):
            if t + 1 < TT:
                emit_S(t + 1)
            if t == TT // 2:
                head_front(j + 1)
            diag_j = t - (TT - 4)
            qlo = qlo_of(t)
            w = 512 - qlo
            bank = stps[t % 2]
            if diag_j >= 0:
                P_.wait("dve", "pe", st[("st_pe", which, H, t)])
                P_.do("dve", lambda e, bank=bank, qlo=qlo: e.tensor_tensor(
                    bank[:, qlo:qlo + 128], bank[:, qlo:qlo + 128], T["tri"][:], ALU.add),
                    [("dve", 1)])
                st[("tri_dve", which, H, t)] = P_.c["dve"]
            P_.wait("act", "pe", st[("st_pe", which, H, t)])
            if diag_j >= 0:
                P_.wait("act", "dve", st[("tri_dve", which, H, t)])
            if t == 0:
                P_.wait("act", "dve", negm_dve)
            P_.wait("act", "dve", st.get(("ptl_dve", which, H, t - 4), 0))
            P_.do("act", lambda e, t=t, bank=bank, qlo=qlo, w=w, negm_col=negm_col:
                  e.activation(
                      pt32[t % 4][:, 0:w], bank[:, qlo:512], AF.Exp,
                      bias=T["mb"][:, negm_col:negm_col + 1], scale=SCALE), [("act", 1)])
            st[("exp_act", which, H, t)] = P_.c["act"]
            P_.wait("act", "pe", st.get(("lo_pe", which, H, t - 4), 0))
            pth, ptl = ptP[t % 4]
            P_.do("act", lambda e, t=t, w=w, pth=pth, voff=voff: e.activation(
                pth[:, 0:w], pt32[t % 4][:, 0:w], AF.Copy,
                scale=T["validsc"][:, voff + t:voff + t + 1]), [("act", 1)])
            st[("pth_act", which, H, t)] = P_.c["act"]
            P_.wait("pe", "act", P_.c["act"])
            vhi, vlo = v16_tile(T, t)
            P_.do("pe", lambda e, t=t, qlo=qlo, w=w, sps=sps, pth=pth: e.matmul(
                sps[0:1, qlo:512], T["ones16"][:], pth[:, 0:w],
                start=(t == 0), stop=False))
            P_.do("pe", lambda e, t=t, qlo=qlo, w=w, gl=gl, ctxps=ctxps, vhi=vhi,
                  pth=pth: e.matmul(
                      ctxps[:, qlo:512], vhi[:, 128 * gl:128 * (gl + 1)], pth[:, 0:w],
                      start=(t == 0), stop=False))
            P_.do("pe", lambda e, t=t, qlo=qlo, w=w, gl=gl, ctxps=ctxps, vlo=vlo,
                  pth=pth: e.matmul(
                      ctxps[:, qlo:512], vlo[:, 128 * gl:128 * (gl + 1)], pth[:, 0:w],
                      start=False, stop=False), [("pe", 1)])
            st[("av_pe", which, H, t)] = P_.c["pe"]
            if t >= 2:
                emit_lo(t - 2)
        emit_lo(TT - 2)
        emit_lo(TT - 1)

        P_.wait("dve", "pe", P_.c["pe"])
        P_.do("dve", lambda e, sps=sps: e.reciprocal(T["rowA"][:], sps[0:1, 0:512]),
              [("dve", 1)])
        P_.wait("pe", "dve", P_.c["dve"])
        P_.do("pe", lambda e, qbank=qbank: e.matmul(
            qbank[:], T["ones1"][:], T["rowA"][:], start=True, stop=True), [("pe", 1)])
        P_.wait("dve", "pe", P_.c["pe"])
        P_.do("dve", lambda e, qbank=qbank: e.tensor_copy(T["rsb"][:], qbank[:]))
        P_.do("dve", lambda e, H=H, ctxps=ctxps: e.tensor_tensor(
            T["ctmp"][:], ctxps[:], T["rsb"][:], ALU.mult), [("dve", 1)])
        P_.wait("act", "dve", P_.c["dve"])
        P_.do("act", lambda e, H=H, ch_=ch_: e.activation(ch_[:, H, :], T["ctmp"][:],
                                                          AF.Copy), [("act", 1)])
        P_.wait("dve", "act", P_.c["act"])
        P_.do("dve", lambda e, H=H, ch_=ch_, cl_=cl_: e.scalar_tensor_tensor(
            cl_[:, H, :], T["ctmp"][:], 1.0, ch_[:, H, :], ALU.mult, ALU.subtract),
            [("dve", 1)])
        st[("ctx_dve", which, H)] = P_.c["dve"]
        if which == "A":
            st["stats_pump"](10)


def wo_ln1(P_, T, dr, st, which):
    P_.barrier()
    str16 = T["aSTR"][:].bitcast(F16)
    woD = T["aSTR"][:, 0:2048]
    woD3 = woD.rearrange("p (c n) -> p c n", n=128)
    woPh = str16[:, 4096:8192].rearrange("p (c n) -> p c n", n=256)[:, :, 0:128]
    woPl = str16[:, 4096:8192].rearrange("p (c n) -> p c n", n=256)[:, :, 128:256]
    v32 = T["aKY"].rearrange("p (c n) -> p c n", n=512)
    ch_, cl_ = ctx16(T)
    xh, xl = xq16(T)

    for do in range(NCH):
        P_.wait("sp", "act", st.get(("split_hi", ("wo", which, do - 1)), 0))
        P_.wait("sp", "dve", st.get(("split_lo", ("wo", which, do - 1)), 0))
        P_.do("sp", lambda e, do=do: e.dma_start(
            out=woD3, in_=dr["wo"][do].rearrange("(c p) m -> p c m", p=128)), [("ld", 16)])
        P_.wait("act", "ld", P_.c["ld"])
        P_.wait("act", "pe", st.get(("wo_pe", which, do - 1), 0))
        P_.wait("dve", "pe", st.get(("wo_pe", which, do - 1), 0))
        split_pair(P_, T, st, woD3, woPh, woPl, ("wo", which, do))
        P_.wait("pe", "act", st[("split_hi", ("wo", which, do))])
        P_.wait("pe", "dve", st[("split_lo", ("wo", which, do))])
        bank = [T["psA"], T["psB"]][do % 2]
        P_.wait("pe", "dve", st.get(("wo_evac", which, do - 2), 0))
        for pi, (wv_, cv_) in enumerate([(woPh, ch_), (woPh, cl_), (woPl, ch_)]):
            for c in range(NCH):
                last = (pi == 2 and c == NCH - 1)
                P_.do("pe", lambda e, c=c, bank=bank, wv_=wv_, cv_=cv_, pi=pi,
                      last=last: e.matmul(
                    bank[:], wv_[:, c, :], cv_[:, c, :],
                    start=(pi == 0 and c == 0), stop=last), [("pe", 1)] if last else ())
        st[("wo_pe", which, do)] = P_.c["pe"]
        P_.wait("dve", "pe", P_.c["pe"])
        P_.do("dve", lambda e, do=do, bank=bank, v32=v32: e.scalar_tensor_tensor(
            v32[:, do, :], bank[:], T["bo"][:, do:do + 1], xh[:, do, :],
            ALU.add, ALU.add))
        P_.do("dve", lambda e, do=do, v32=v32: e.tensor_tensor(
            v32[:, do, :], v32[:, do, :], xl[:, do, :], ALU.add), [("dve", 1)])
        st[("wo_evac", which, do)] = P_.c["dve"]
        P_.wait("pe", "dve", P_.c["dve"])
        P_.do("pe", lambda e, do=do, v32=v32: e.matmul(
            T["psS1"][0:1, 0:512], T["mb"][:, MB_ONESC:MB_ONESC + 1], v32[:, do, :],
            start=(do == 0), stop=(do == NCH - 1)), [("pe", 1)])
        P_.wait("act", "pe", P_.c["pe"])
        P_.wait("act", "pe", st.get(("sq_mm_pe", which, do - 1), 0))
        P_.do("act", lambda e, do=do, v32=v32: e.activation(T["scr"][:], v32[:, do, :],
                                                            AF.Square), [("act", 1)])
        P_.wait("pe", "act", P_.c["act"])
        P_.do("pe", lambda e, do=do: e.matmul(
            T["psS2"][0:1, 0:512], T["mb"][:, MB_ONESC:MB_ONESC + 1], T["scr"][:],
            start=(do == 0), stop=(do == NCH - 1)), [("pe", 1)])
        st[("sq_mm_pe", which, do)] = P_.c["pe"]

    ky16 = T["aKY"][:].bitcast(F16).rearrange("p (c n) -> p c n", n=1024)
    y1h = ky16[:, :, 0:512]
    y1l = ky16[:, :, 512:1024]
    ln_norm_pair(P_, T, st, v32, (y1h, y1l), which + "ln1")


def ln_norm(P_, T, st, src_v, dst_v, tag, pair=None):
    P_.wait("dve", "pe", P_.c["pe"])
    P_.do("dve", lambda e: e.tensor_scalar(T["rowB"][:], T["psS1"][0:1, 0:512],
                                           1.0 / D, None, ALU.mult))
    P_.do("dve", lambda e: e.tensor_scalar(T["rowA"][:], T["psS2"][0:1, 0:512],
                                           1.0 / D, None, ALU.mult))
    P_.do("dve", lambda e: e.tensor_tensor(T["rowE"][:], T["rowB"][:],
                                           T["rowB"][:], ALU.mult))
    P_.do("dve", lambda e: e.tensor_tensor(T["rowA"][:], T["rowA"][:],
                                           T["rowE"][:], ALU.subtract), [("dve", 1)])
    P_.wait("act", "dve", P_.c["dve"])
    P_.do("act", lambda e: e.activation(T["rowA"][:], T["rowA"][:], AF.Sqrt,
                                        bias=T["rowD"][0:1, 48:49]), [("act", 1)])
    P_.wait("dve", "act", P_.c["act"])
    P_.do("dve", lambda e: e.reciprocal(T["rowC"][:], T["rowA"][:]), [("dve", 1)])
    P_.wait("pe", "dve", P_.c["dve"])
    P_.do("pe", lambda e: e.matmul(T["psE"][:], T["ones1"][:], T["rowB"][:],
                                   start=True, stop=True))
    P_.do("pe", lambda e: e.matmul(T["psF"][:], T["ones1"][:], T["rowC"][:],
                                   start=True, stop=True), [("pe", 1)])
    P_.wait("dve", "pe", P_.c["pe"])
    for do in range(NCH):
        P_.do("dve", lambda e, do=do: e.tensor_tensor(
            dst_v[:, do, :], src_v[:, do, :], T["psE"][:], ALU.subtract))
        P_.do("dve", lambda e, do=do: e.tensor_tensor(
            dst_v[:, do, :], dst_v[:, do, :], T["psF"][:], ALU.mult), [("dve", 1)])
        if pair is not None:
            hi, lo = pair
            P_.do("dve", lambda e, do=do, hi=hi: e.tensor_copy(hi[:, do, :],
                                                               dst_v[:, do, :]))
            P_.do("dve", lambda e, do=do, hi=hi, lo=lo: e.scalar_tensor_tensor(
                lo[:, do, :], dst_v[:, do, :], 1.0, hi[:, do, :],
                ALU.mult, ALU.subtract), [("dve", 1)])
    st[("ln_done_dve", tag)] = P_.c["dve"]


def ln_norm_pair(P_, T, st, src_v, pair, tag):
    hi, lo = pair
    P_.wait("dve", "pe", P_.c["pe"])
    P_.do("dve", lambda e: e.tensor_scalar(T["rowB"][:], T["psS1"][0:1, 0:512],
                                           1.0 / D, None, ALU.mult))
    P_.do("dve", lambda e: e.tensor_scalar(T["rowA"][:], T["psS2"][0:1, 0:512],
                                           1.0 / D, None, ALU.mult))
    P_.do("dve", lambda e: e.tensor_tensor(T["rowE"][:], T["rowB"][:],
                                           T["rowB"][:], ALU.mult))
    P_.do("dve", lambda e: e.tensor_tensor(T["rowA"][:], T["rowA"][:],
                                           T["rowE"][:], ALU.subtract), [("dve", 1)])
    P_.wait("act", "dve", P_.c["dve"])
    P_.do("act", lambda e: e.activation(T["rowA"][:], T["rowA"][:], AF.Sqrt,
                                        bias=T["rowD"][0:1, 48:49]), [("act", 1)])
    P_.wait("dve", "act", P_.c["act"])
    P_.do("dve", lambda e: e.reciprocal(T["rowC"][:], T["rowA"][:]), [("dve", 1)])
    P_.wait("pe", "dve", P_.c["dve"])
    P_.do("pe", lambda e: e.matmul(T["psE"][:], T["ones1"][:], T["rowB"][:],
                                   start=True, stop=True))
    P_.do("pe", lambda e: e.matmul(T["psF"][:], T["ones1"][:], T["rowC"][:],
                                   start=True, stop=True), [("pe", 1)])
    P_.wait("dve", "pe", P_.c["pe"])
    tmp = T["rsb"]
    for do in range(NCH):
        P_.do("dve", lambda e, do=do, tmp=tmp: e.tensor_tensor(
            tmp[:], src_v[:, do, :], T["psE"][:], ALU.subtract))
        P_.do("dve", lambda e, do=do, tmp=tmp: e.tensor_tensor(
            tmp[:], tmp[:], T["psF"][:], ALU.mult))
        P_.do("dve", lambda e, do=do, hi=hi, tmp=tmp: e.tensor_copy(hi[:, do, :], tmp[:]))
        P_.do("dve", lambda e, do=do, hi=hi, lo=lo, tmp=tmp: e.scalar_tensor_tensor(
            lo[:, do, :], tmp[:], 1.0, hi[:, do, :], ALU.mult, ALU.subtract),
            [("dve", 1)])
    st[("ln_done_dve", tag)] = P_.c["dve"]


def ffn(P_, T, dr, st, which):
    P_.barrier()
    ky16 = T["aKY"][:].bitcast(F16).rearrange("p (c n) -> p c n", n=1024)
    y1h = ky16[:, :, 0:512]
    y1l = ky16[:, :, 512:1024]
    v2 = T["aXQ"].rearrange("p (c n) -> p c n", n=512)
    v16 = T["aV"][:].bitcast(F16).rearrange("p (c n) -> p c n", n=1024)
    h2h = v16[:, :, 0:512]
    h2l = v16[:, :, 512:1024]
    wfcb = [T["aSTR"][:, 2048 * k:2048 * (k + 1)] for k in range(4)]
    qfc_sl = [T["aSTR"][:, 8192 + 1024 * k:9216 + 1024 * k].bitcast(F16) for k in range(2)]
    wprb = [T["aCTX"][:, 1280 * k:1280 * (k + 1)] for k in range(4)]
    qpr_sl = [T["aCTX"][:, 5120 + 640 * k:5760 + 640 * k].bitcast(F16) for k in range(2)]
    scr2 = T["aCTX"][:, 6400:6912]
    cdve = st["ffn_consts_dve"]

    for q in range(NQUARTER):
        for hc in range(HH_PER_Q):
            hh = q * HH_PER_Q + hc
            P_.wait("sp", "dve", st.get(("tern_fc", which, hh - 4), 0))
            P_.do("sp", lambda e, hh=hh: e.dma_start(
                out=wfcb[hh % 4].rearrange("p (c n) -> p c n", n=128),
                in_=dr["wfc"][hh].rearrange("(c p) m -> p c m", p=128)), [("ld", 16)])
            qfc = qfc_sl[hh % 2]
            P_.wait("dve", "ld", P_.c["ld"])
            P_.wait("dve", "pe", st.get(("mm1_pe", which, hh - 2), 0))
            P_.do("dve", lambda e, qfc=qfc, hh=hh: e.tensor_scalar(
                qfc[:], wfcb[hh % 4][:], T["mb"][:, MB_NTHRF:MB_NTHRF + 1], None, ALU.is_lt))
            P_.do("dve", lambda e, qfc=qfc, hh=hh: e.scalar_tensor_tensor(
                qfc[:], wfcb[hh % 4][:], T["mb"][:, MB_THRF:MB_THRF + 1], qfc[:],
                ALU.is_gt, ALU.subtract), [("dve", 1)])
            st[("tern_fc", which, hh)] = P_.c["dve"]
            P_.wait("pe", "dve", P_.c["dve"])
            bank = [T["psA"], T["psB"]][hc % 2]
            P_.wait("pe", "act", st.get(("h_evac", which, q, hc - 2), 0))
            qv = qfc.rearrange("p (c n) -> p c n", n=128)
            for half in range(2):
                yv = y1h if half == 0 else y1l
                for c in range(NCH):
                    P_.do("pe", lambda e, c=c, bank=bank, qv=qv, yv=yv: e.matmul(
                        bank[:], qv[:, c, :], yv[:, c, :],
                        start=(half == 0 and c == 0), stop=(half == 1 and c == NCH - 1)),
                        [("pe", 1)] if (half == 1 and c == NCH - 1) else ())
            st[("mm1_pe", which, hh)] = P_.c["pe"]
            P_.wait("act", "pe", P_.c["pe"])
            if hc >= 1 or q >= 1:
                P_.wait("act", "dve", st.get(("h2l_dve", which, q * HH_PER_Q + hc - 1), 0))
            P_.do("act", lambda e, bank=bank: e.activation(T["scr"][:], bank[:], AF.Relu))
            P_.do("act", lambda e, scr2=scr2: e.activation(scr2[:], T["scr"][:], AF.Square,
                                                           scale=0.0625))
            P_.do("act", lambda e, hc=hc, scr2=scr2: e.activation(h2h[:, hc, :], scr2[:],
                                                                  AF.Copy), [("act", 1)])
            st[("h_evac", which, q, hc)] = P_.c["act"]
            P_.wait("dve", "act", P_.c["act"])
            P_.do("dve", lambda e, hc=hc, scr2=scr2: e.scalar_tensor_tensor(
                h2l[:, hc, :], scr2[:], 1.0, h2h[:, hc, :], ALU.mult, ALU.subtract),
                [("dve", 1)])
            st[("h2l_dve", which, hh)] = P_.c["dve"]

        for do in range(NCH):
            dd = q * NCH + do
            P_.wait("sp", "dve", st.get(("tern_pr", which, dd - 4), 0))
            P_.do("sp", lambda e, do=do, q=q, dd=dd: e.dma_start(
                out=wprb[dd % 4].rearrange("p (c n) -> p c n", n=128),
                in_=dr["wpr"][do][1280 * q:1280 * (q + 1), :].rearrange(
                    "(c p) m -> p c m", p=128)), [("ld", 16)])
            P_.wait("dve", "ld", P_.c["ld"])
            if do == 0:
                P_.wait("dve", "dve", st[("h2l_dve", which, q * HH_PER_Q + HH_PER_Q - 1)])
                P_.wait("dve", "act", st[("h_evac", which, q, HH_PER_Q - 1)])
            qpr = qpr_sl[dd % 2]
            P_.wait("dve", "pe", st.get(("mm2_pe", which, dd - 2), 0))
            P_.do("dve", lambda e, qpr=qpr, dd=dd: e.tensor_scalar(
                qpr[:], wprb[dd % 4][:], T["mb"][:, MB_NTHRP:MB_NTHRP + 1], None, ALU.is_lt))
            P_.do("dve", lambda e, qpr=qpr, dd=dd: e.scalar_tensor_tensor(
                qpr[:], wprb[dd % 4][:], T["mb"][:, MB_THRP:MB_THRP + 1], qpr[:],
                ALU.is_gt, ALU.subtract), [("dve", 1)])
            st[("tern_pr", which, dd)] = P_.c["dve"]
            P_.wait("pe", "dve", P_.c["dve"])
            bank = [T["psC"], T["psD"]][dd % 2]
            P_.wait("pe", "dve", st.get(("v2_evac", which, dd - 2), 0))
            qprv = qpr.rearrange("p (c n) -> p c n", n=128)
            for half in range(2):
                hv = h2h if half == 0 else h2l
                for hc in range(HH_PER_Q):
                    P_.do("pe", lambda e, hc=hc, bank=bank, qprv=qprv, hv=hv: e.matmul(
                        bank[:], qprv[:, hc, :], hv[:, hc, :],
                        start=(half == 0 and hc == 0),
                        stop=(half == 1 and hc == HH_PER_Q - 1)),
                        [("pe", 1)] if (half == 1 and hc == HH_PER_Q - 1) else ())
            st[("mm2_pe", which, dd)] = P_.c["pe"]
            P_.wait("dve", "pe", P_.c["pe"])
            if q == 0:
                P_.do("dve", lambda e, do=do, bank=bank, v2=v2: e.scalar_tensor_tensor(
                    v2[:, do, :], bank[:], T["mb"][:, MB_CC:MB_CC + 1], y1h[:, do, :],
                    ALU.mult, ALU.add))
                P_.do("dve", lambda e, do=do, v2=v2: e.tensor_tensor(
                    v2[:, do, :], v2[:, do, :], y1l[:, do, :], ALU.add), [("dve", 1)])
            else:
                P_.do("dve", lambda e, do=do, bank=bank, v2=v2: e.scalar_tensor_tensor(
                    v2[:, do, :], bank[:], T["mb"][:, MB_CC:MB_CC + 1], v2[:, do, :],
                    ALU.mult, ALU.add), [("dve", 1)])
            st[("v2_evac", which, dd)] = P_.c["dve"]


def ln2_out(P_, T, dr, st, which, out_col):
    P_.barrier()
    v2 = T["aXQ"].rearrange("p (c n) -> p c n", n=512)
    for do in range(NCH):
        P_.do("pe", lambda e, do=do, v2=v2: e.matmul(
            T["psS1"][0:1, 0:512], T["mb"][:, MB_ONESC:MB_ONESC + 1], v2[:, do, :],
            start=(do == 0), stop=(do == NCH - 1)), [("pe", 1)])
        P_.wait("act", "pe", P_.c["pe"])
        P_.wait("act", "pe", st.get(("sq2_mm_pe", which, do - 1), 0))
        P_.do("act", lambda e, do=do, v2=v2: e.activation(T["scr"][:], v2[:, do, :],
                                                          AF.Square), [("act", 1)])
        P_.wait("pe", "act", P_.c["act"])
        P_.do("pe", lambda e, do=do: e.matmul(
            T["psS2"][0:1, 0:512], T["mb"][:, MB_ONESC:MB_ONESC + 1], T["scr"][:],
            start=(do == 0), stop=(do == NCH - 1)), [("pe", 1)])
        st[("sq2_mm_pe", which, do)] = P_.c["pe"]

    ln_norm(P_, T, st, v2, v2, which + "ln2")
    P_.wait("act", "dve", P_.c["dve"])
    for do in range(NCH):
        P_.do("act", lambda e, do=do, v2=v2: e.dma_start(
            out=dr["y2"][128 * do:128 * (do + 1), out_col:out_col + CH],
            in_=v2[:, do, :]), [("so", 16)])
    P_.wait("sp", "so", P_.c["so"])



def kernel(L0_attn_x, L0_attn_Wq, L0_attn_Wk, L0_attn_Wv, L0_attn_Wo, L0_attn_bo,
           L0_ffn_w_fc, L0_ffn_w_proj):
    x = np.asarray(L0_attn_x, dtype=np.float32)
    Wq = np.asarray(L0_attn_Wq, dtype=np.float32)
    Wk = np.asarray(L0_attn_Wk, dtype=np.float32)
    Wv = np.asarray(L0_attn_Wv, dtype=np.float32)
    Wo = np.asarray(L0_attn_Wo, dtype=np.float32)
    bo = np.asarray(L0_attn_bo, dtype=np.float32)
    Wfc = np.asarray(L0_ffn_w_fc, dtype=np.float32)
    Wpr = np.asarray(L0_ffn_w_proj, dtype=np.float32)

    wq_blk = np.ascontiguousarray(Wq.reshape(D, HEADS, 128).transpose(1, 0, 2))
    wk_blk = np.ascontiguousarray(Wk.reshape(D, NGRP, 128).transpose(1, 0, 2))
    wo_blk = np.ascontiguousarray(Wo.reshape(D, NCH, 128).transpose(1, 0, 2))
    bo_t = np.ascontiguousarray(bo.reshape(NCH, 128).T)
    wfc_blk = np.ascontiguousarray(Wfc.T.reshape(D, NHH, 128).transpose(1, 0, 2))
    wpr_blk = np.ascontiguousarray(Wpr.T.reshape(HFF, NCH, 128).transpose(1, 0, 2))
    tri = np.where(np.arange(128)[:, None] <= np.arange(128)[None, :],
                   np.float32(0.0), np.float32(NEG_BIG)).astype(np.float32)

    in_maps = []
    for core in range(8):
        b, i = divmod(core, 4)
        xt = np.ascontiguousarray(x[b].T)
        cA, cB = i, 4 + i
        pA, pB = CH * cA, CH * cB

        xkvA = np.zeros((D, PA), dtype=np.float32)
        xkvA[:, :pA] = xt[:, :pA]
        xkvA[:, PA - CH:] = xt[:, CH * cA:CH * (cA + 1)]
        xkvB = np.zeros((D, PB), dtype=np.float32)
        xkvB[:, :pB] = xt[:, :pB]
        xkvB[:, PB - CH:] = xt[:, CH * cB:CH * (cB + 1)]

        def mk_valid(Pn, p_real):
            idx = np.arange(Pn)
            v = ((idx < p_real) | (idx >= Pn - CH)).astype(np.float32)
            return np.ascontiguousarray(v.reshape(Pn // 128, 128).T)

        in_maps.append({
            "xkvA": xkvA, "xkvB": xkvB,
            "validA": mk_valid(PA, pA), "validB": mk_valid(PB, pB),
            "tri": tri,
            "wq": wq_blk, "wk": wk_blk, "wv": Wv, "wo": wo_blk, "bo": bo_t,
            "wfc": wfc_blk, "wpr": wpr_blk,
        })

    nc = build_program()
    res = run_bass_kernel_spmd(nc, in_maps, core_ids=list(range(8)))

    yt = np.empty((2, D, SEQ), dtype=np.float32)
    for core in range(8):
        b, i = divmod(core, 4)
        o = res.results[core]["y2"]
        yt[b][:, CH * i:CH * (i + 1)] = o[:, 0:CH]
        yt[b][:, CH * (4 + i):CH * (5 + i)] = o[:, CH:2 * CH]
    return np.ascontiguousarray(yt.transpose(0, 2, 1))


# version 11
# speedup vs baseline: 1.0253x; 1.0253x over previous
import numpy as np
import concourse.bass as bass
import concourse.mybir as mybir
from concourse.bass_utils import run_bass_kernel_spmd

F32 = mybir.dt.float32
F16 = mybir.dt.float16
AF = mybir.ActivationFunctionType
ALU = mybir.AluOpType
AX = mybir.AxisListType

D = 2048
NCH = 16
HEADS = 16
NGRP = 4
SEQ = 4096
CH = 512
PA, PB = 2048, 4096
TA, TB = PA // 128, PB // 128
HFF = 5120
NHH = 40
NQUARTER = 4
HH_PER_Q = 10
SCALE = float(1.0 / np.sqrt(np.float32(128.0)))
NEG_BIG = -1.0e30
EPS = 1e-5

STAT_CHUNK = 512
N_STAT_FC = (NHH * D * 128) // (128 * STAT_CHUNK)
N_STAT_PR = (NCH * HFF * 128) // (128 * STAT_CHUNK)
N_STAT = N_STAT_FC + N_STAT_PR

MB_ONESC = 0
MB_NEGM = 1
MB_THRF = 2
MB_NTHRF = 3
MB_THRP = 4
MB_NTHRP = 5
MB_CC = 6
MB_NEGM2 = 7
MB_RED0 = 8
MB_KMAX = 10

SR_KN = 0
SR_RS = 1
SR_MEAN = 2
SR_RSTD = 3
SR_TMP = 4
SR_TMP2 = 5
SR_M11 = 6
SR_NEGMROW = 7


class P:

    def __init__(self, nc, sems, eng, handles):
        self.nc = nc
        self.sems = sems
        self.eng = eng
        self.h = handles
        self.c = {s: 0 for s in sems}

    def do(self, eng, fn, incs=()):
        if self.eng == eng:
            inst = fn(self.h[eng])
            for s, n in incs:
                inst.then_inc(self.sems[s], n)
        for s, n in incs:
            self.c[s] += n

    def wait(self, eng, sem, val):
        assert val <= self.c[sem], \
            f"unsatisfiable wait {eng} {sem}>={val} (planned {self.c[sem]})"
        if val > 0 and self.eng == eng:
            self.h[eng].wait_ge(self.sems[sem], val)

    def barrier(self):
        snap = {e: self.c[e] for e in ("pe", "act", "dve")}
        for e in ("pe", "act", "dve", "sp"):
            for o in ("pe", "act", "dve"):
                if o != e:
                    self.wait(e, o, snap[o])


def srow(T, r, n=512):
    return T["stats"][r:r + 1, 0:n]


def build_program():
    nc = bass.Bass("TRN2", target_bir_lowering=False)

    dr = {}
    dr["xkvA"] = nc.dram_tensor("xkvA", [D, PA], F32, kind="ExternalInput").ap()
    dr["xkvB"] = nc.dram_tensor("xkvB", [D, PB], F32, kind="ExternalInput").ap()
    dr["validA"] = nc.dram_tensor("validA", [128, TA], F32, kind="ExternalInput").ap()
    dr["validB"] = nc.dram_tensor("validB", [128, TB], F32, kind="ExternalInput").ap()
    dr["tri"] = nc.dram_tensor("tri", [128, 128], F32, kind="ExternalInput").ap()
    dr["wq"] = nc.dram_tensor("wq", [HEADS, D, 128], F32, kind="ExternalInput").ap()
    dr["wk"] = nc.dram_tensor("wk", [NGRP, D, 128], F32, kind="ExternalInput").ap()
    dr["wv"] = nc.dram_tensor("wv", [D, 512], F32, kind="ExternalInput").ap()
    dr["wo"] = nc.dram_tensor("wo", [NCH, D, 128], F32, kind="ExternalInput").ap()
    dr["bo"] = nc.dram_tensor("bo", [128, NCH], F32, kind="ExternalInput").ap()
    dr["wfc"] = nc.dram_tensor("wfc", [NHH, D, 128], F32, kind="ExternalInput").ap()
    dr["wpr"] = nc.dram_tensor("wpr", [NCH, HFF, 128], F32, kind="ExternalInput").ap()
    dr["y2"] = nc.dram_tensor("y2", [D, 2 * CH], F32, kind="ExternalOutput").ap()

    from contextlib import ExitStack
    with ExitStack() as _es:
        def A(cm):
            return _es.enter_context(cm)
        aKY = A(nc.sbuf_tensor("aKY", [128, 8192], F32))
        aV = A(nc.sbuf_tensor("aV", [128, 8192], F32))
        aXQ = A(nc.sbuf_tensor("aXQ", [128, 8192], F32))
        aCTX = A(nc.sbuf_tensor("aCTX", [128, 8192], F32))
        aSTR = A(nc.sbuf_tensor("aSTR", [128, 12288], F32))
        tri_sb = A(nc.sbuf_tensor("tri_sb", [128, 128], F32))
        valid_sb = A(nc.sbuf_tensor("valid_sb", [128, TA + TB], F32))
        bo_sb = A(nc.sbuf_tensor("bo_sb", [128, NCH], F32))
        stats_sb = A(nc.sbuf_tensor("stats_sb", [128, 512], F32))
        scr = A(nc.sbuf_tensor("scr", [128, 512], F32))
        rsb = A(nc.sbuf_tensor("rsb", [128, 512], F32))
        mb = A(nc.sbuf_tensor("mb", [128, 64], F32))
        ones1 = A(nc.sbuf_tensor("ones1", [1, 128], F32))
        rowA = A(nc.sbuf_tensor("rowA", [1, 512], F32))
        rowB = A(nc.sbuf_tensor("rowB", [1, 512], F32))
        rowC = A(nc.sbuf_tensor("rowC", [1, 512], F32))
        rowD = A(nc.sbuf_tensor("rowD", [1, 64], F32))
        rowE = A(nc.sbuf_tensor("rowE", [1, 512], F32))
        ones16 = A(nc.sbuf_tensor("ones16", [128, 1], F16))
        ctmp = A(nc.sbuf_tensor("ctmp", [128, 512], F32))
        validsc = A(nc.sbuf_tensor("validsc", [128, TA + TB], F32))
        psA = A(nc.psum_tensor("psA", [128, 512], F32))
        psB = A(nc.psum_tensor("psB", [128, 512], F32))
        psC = A(nc.psum_tensor("psC", [128, 512], F32))
        psD = A(nc.psum_tensor("psD", [128, 512], F32))
        psE = A(nc.psum_tensor("psE", [128, 512], F32))
        psF = A(nc.psum_tensor("psF", [128, 512], F32))
        psS1 = A(nc.psum_tensor("psS1", [1, 512], F32))
        psS2 = A(nc.psum_tensor("psS2", [1, 512], F32))
        s_ld = A(nc.semaphore("ld"))
        s_gd = A(nc.semaphore("gd"))
        s_so = A(nc.semaphore("so"))
        s_pe = A(nc.semaphore("pe"))
        s_act = A(nc.semaphore("act"))
        s_dve = A(nc.semaphore("dve"))
        block = A(nc.Block())
        sems = {"ld": s_ld, "gd": s_gd, "so": s_so,
                "pe": s_pe, "act": s_act, "dve": s_dve}
        T = {
            "aKY": aKY, "aV": aV, "aXQ": aXQ, "aCTX": aCTX, "aSTR": aSTR,
            "tri": tri_sb, "valid": valid_sb, "bo": bo_sb,
            "stats": stats_sb, "scr": scr, "rsb": rsb, "mb": mb, "ones1": ones1,
            "rowA": rowA, "rowB": rowB, "rowC": rowC, "rowD": rowD, "rowE": rowE,
            "ones16": ones16, "ctmp": ctmp, "validsc": validsc,
            "psA": psA, "psB": psB, "psC": psC, "psD": psD,
            "psE": psE, "psF": psF, "psS1": psS1, "psS2": psS2,
        }

        @block.tensor
        def _(tensor):
            program(P(nc, sems, "pe", {"pe": tensor}), T, dr)

        @block.scalar
        def _(scalar):
            program(P(nc, sems, "act", {"act": scalar}), T, dr)

        @block.vector
        def _(vector):
            program(P(nc, sems, "dve", {"dve": vector}), T, dr)

        @block.sync
        def _(sync):
            program(P(nc, sems, "sp", {"sp": sync}), T, dr)

        @block.gpsimd
        def _(gpsimd):
            program(P(nc, sems, "gp", {"gp": gpsimd}), T, dr)

    return nc



def program(P_, T, dr):
    st = {}
    prologue(P_, T, dr, st)
    stats_plan(P_, T, dr, st)
    macro(P_, T, dr, st, "A")
    macro(P_, T, dr, st, "B")
    P_.wait("sp", "so", P_.c["so"])


def prologue(P_, T, dr, st):
    P_.do("sp", lambda e: e.dma_start(out=T["tri"][:], in_=dr["tri"][:]), [("ld", 16)])
    P_.do("sp", lambda e: e.dma_start(out=T["valid"][:, 0:TA], in_=dr["validA"][:]), [("ld", 16)])
    P_.do("sp", lambda e: e.dma_start(out=T["valid"][:, TA:TA + TB], in_=dr["validB"][:]), [("ld", 16)])
    P_.do("sp", lambda e: e.dma_start(out=T["bo"][:], in_=dr["bo"][:]), [("ld", 16)])
    st["consts_ld"] = P_.c["ld"]
    P_.do("dve", lambda e: e.memset(T["ones1"][:], 1.0))
    P_.do("dve", lambda e: e.memset(T["mb"][:, MB_ONESC:MB_ONESC + 1], 1.0))
    P_.do("dve", lambda e: e.memset(T["rowD"][0:1, 48:49], EPS))
    P_.do("dve", lambda e: e.memset(T["ones16"][:], 1.0), [("dve", 1)])
    P_.wait("dve", "ld", st["consts_ld"])
    P_.do("dve", lambda e: e.tensor_scalar(T["validsc"][:], T["valid"][:], 16384.0,
                                           None, ALU.mult), [("dve", 1)])
    st["consts_dve"] = P_.c["dve"]
    P_.wait("pe", "dve", st["consts_dve"])
    P_.wait("act", "dve", st["consts_dve"])
    P_.wait("dve", "ld", st["consts_ld"])
    P_.wait("pe", "ld", st["consts_ld"])
    P_.wait("act", "ld", st["consts_ld"])


def stats_plan(P_, T, dr, st):
    ring = [T["aV"][:, 4096 + 512 * j: 4096 + 512 * (j + 1)] for j in range(4)]
    acc = T["aV"][:, 6400:6400 + N_STAT]

    def src_ap(j):
        if j < N_STAT_FC:
            a, q = divmod(j, 4)
            return dr["wfc"][a].rearrange("(q p) m -> p q m", p=128)[:, 4 * q:4 * (q + 1), :]
        j -= N_STAT_FC
        a, q = divmod(j, 10)
        return dr["wpr"][a].rearrange("(q p) m -> p q m", p=128)[:, 4 * q:4 * (q + 1), :]

    st["stats_dma_n"] = 0
    st["stats_abs_n"] = 0
    st["stats_absdone_act"] = []
    st["stats_ready_gd"] = []

    def emit_dma(j):
        if j >= 4:
            P_.wait("gp", "act", st["stats_absdone_act"][j - 4])
        P_.do("gp", lambda e, j=j: e.dma_start(
            out=ring[j % 4].rearrange("p (q m) -> p q m", m=128),
            in_=src_ap(j)), [("gd", 16)])
        st["stats_ready_gd"].append(P_.c["gd"])

    def pump(n):
        target = min(st["stats_abs_n"] + n, N_STAT)
        while st["stats_abs_n"] < target:
            while st["stats_dma_n"] < min(st["stats_abs_n"] + 4, N_STAT):
                emit_dma(st["stats_dma_n"])
                st["stats_dma_n"] += 1
            j = st["stats_abs_n"]
            P_.wait("act", "gd", st["stats_ready_gd"][j])
            P_.do("act", lambda e, j=j: e.activation(
                ring[j % 4][:], ring[j % 4][:], AF.Abs,
                accum_out=acc[:, j:j + 1]), [("act", 1)])
            st["stats_absdone_act"].append(P_.c["act"])
            st["stats_abs_n"] = j + 1

    st["stats_pump"] = pump
    st["stats_acc"] = acc


def stats_finalize(P_, T, dr, st):
    if st.get("stats_final_done"):
        return
    st["stats_pump"](N_STAT)
    acc = st["stats_acc"]
    P_.wait("dve", "act", P_.c["act"])
    P_.do("dve", lambda e: e.reduce_sum(T["mb"][:, MB_RED0:MB_RED0 + 1],
                                        acc[:, 0:N_STAT_FC], axis=AX.X))
    P_.do("dve", lambda e: e.reduce_sum(T["mb"][:, MB_RED0 + 1:MB_RED0 + 2],
                                        acc[:, N_STAT_FC:N_STAT], axis=AX.X), [("dve", 1)])
    P_.wait("pe", "dve", P_.c["dve"])
    P_.do("pe", lambda e: e.matmul(T["psS1"][0:1, 0:1], T["mb"][:, MB_RED0:MB_RED0 + 1],
                                   T["mb"][:, MB_ONESC:MB_ONESC + 1], start=True, stop=True))
    P_.do("pe", lambda e: e.matmul(T["psS2"][0:1, 0:1], T["mb"][:, MB_RED0 + 1:MB_RED0 + 2],
                                   T["mb"][:, MB_ONESC:MB_ONESC + 1], start=True, stop=True),
          [("pe", 1)])
    P_.wait("dve", "pe", P_.c["pe"])
    r = T["rowD"][0:1, 0:16]
    NFC = float(NHH * D * 128)
    NPR = float(NCH * HFF * 128)
    P_.do("dve", lambda e: e.tensor_scalar(r[0:1, 0:1], T["psS1"][0:1, 0:1], 1.0 / NFC, 1e-5,
                                           ALU.mult, ALU.max))
    P_.do("dve", lambda e: e.tensor_scalar(r[0:1, 1:2], T["psS2"][0:1, 0:1], 1.0 / NPR, 1e-5,
                                           ALU.mult, ALU.max))
    P_.do("dve", lambda e: e.tensor_scalar(r[0:1, 2:3], r[0:1, 0:1], 0.7, None, ALU.mult))
    P_.do("dve", lambda e: e.tensor_scalar(r[0:1, 3:4], r[0:1, 1:2], 0.7, None, ALU.mult))
    P_.do("dve", lambda e: e.tensor_scalar(r[0:1, 4:5], r[0:1, 2:3], -1.0, None, ALU.mult))
    P_.do("dve", lambda e: e.tensor_scalar(r[0:1, 8:9], r[0:1, 0:1], 16.0, None, ALU.mult))
    P_.do("dve", lambda e: e.tensor_scalar(r[0:1, 5:6], r[0:1, 3:4], -1.0, None, ALU.mult))
    P_.do("dve", lambda e: e.tensor_scalar(r[0:1, 7:8], r[0:1, 1:2], r[0:1, 8:9],
                                           r[0:1, 8:9], ALU.mult, ALU.mult), [("dve", 1)])
    P_.wait("pe", "dve", P_.c["dve"])
    pairs = [(2, MB_THRF), (4, MB_NTHRF), (3, MB_THRP), (5, MB_NTHRP), (7, MB_CC)]
    for k, (src, dstc) in enumerate(pairs):
        P_.do("pe", lambda e, src=src, dstc=dstc: e.matmul(
            T["psE"][:, dstc:dstc + 1], T["ones1"][:], r[0:1, src:src + 1],
            start=True, stop=True), [("pe", 1)] if k == len(pairs) - 1 else ())
    P_.wait("dve", "pe", P_.c["pe"])
    P_.do("dve", lambda e: e.tensor_copy(T["mb"][:, MB_THRF:MB_CC + 1],
                                         T["psE"][:, MB_THRF:MB_CC + 1]), [("dve", 1)])
    st["ffn_consts_dve"] = P_.c["dve"]
    st["stats_final_done"] = True



def macro(P_, T, dr, st, which):
    TT = TA if which == "A" else TB
    xkv = dr["xkvA"] if which == "A" else dr["xkvB"]
    voff = 0 if which == "A" else TA
    out_col = 0 if which == "A" else CH

    for gp_i in (0, 1):
        kv_pass(P_, T, dr, st, which, gp_i, TT, xkv)
        if gp_i == 0:
            split_xq(P_, T, dr, st, which)
        attn(P_, T, dr, st, which, gp_i, TT, voff)
    wo_ln1(P_, T, dr, st, which)
    stats_finalize(P_, T, dr, st)
    ffn(P_, T, dr, st, which)
    ln2_out(P_, T, dr, st, which, out_col)


def k16_tile(T, gl, t128):
    t256, sub = divmod(t128, 2)
    base = 8192 * gl + 512 * t256
    k16 = T["aKY"][:].bitcast(F16)
    hi = k16[:, base + 128 * sub: base + 128 * sub + 128]
    lo = k16[:, base + 256 + 128 * sub: base + 256 + 128 * sub + 128]
    return hi, lo


def v16_tile(T, t128):
    v16 = T["aV"][:].bitcast(F16)
    base = 512 * t128
    return v16[:, base:base + 256], v16[:, base + 256:base + 512]


def xq16(T):
    x16 = T["aXQ"][:].bitcast(F16).rearrange("p (c n) -> p c n", n=1024)
    return x16[:, :, 0:512], x16[:, :, 512:1024]


def ctx16(T):
    c16 = T["aCTX"][:].bitcast(F16).rearrange("p (c n) -> p c n", n=1024)
    return c16[:, :, 0:512], c16[:, :, 512:1024]


def split_pair(P_, T, st, src32, hi, lo, tag):
    P_.do("act", lambda e: e.activation(hi, src32, AF.Copy), [("act", 1)])
    st[("split_hi", tag)] = P_.c["act"]
    P_.wait("dve", "act", P_.c["act"])
    P_.do("dve", lambda e: e.scalar_tensor_tensor(lo, src32, 1.0, hi,
                                                  ALU.mult, ALU.subtract), [("dve", 1)])
    st[("split_lo", tag)] = P_.c["dve"]


def kv_pass(P_, T, dr, st, which, gp_i, TT, xkv):
    P_.barrier()
    T256 = TT // 2
    voff = 0 if which == "A" else TA
    str16 = T["aSTR"][:].bitcast(F16)
    wkPh = str16[:, 0:8192].rearrange("p (c n) -> p c n", n=512)[:, :, 0:256]
    wkPl = str16[:, 0:8192].rearrange("p (c n) -> p c n", n=512)[:, :, 256:512]
    wvPh = str16[:, 8192:16384].rearrange("p (c n) -> p c n", n=512)[:, :, 0:256]
    wvPl = str16[:, 8192:16384].rearrange("p (c n) -> p c n", n=512)[:, :, 256:512]
    xPh = str16[:, 16384:24576].rearrange("p (c n) -> p c n", n=512)[:, :, 0:256]
    xPl = str16[:, 16384:24576].rearrange("p (c n) -> p c n", n=512)[:, :, 256:512]
    wdst = T["aCTX"][:, 4096:8192]
    wdst3 = wdst.rearrange("p (c n) -> p c n", n=256)

    for gl in range(2):
        g = 2 * gp_i + gl
        P_.do("sp", lambda e, g=g, gl=gl: e.dma_start(
            out=wdst3[:, :, 128 * gl:128 * (gl + 1)],
            in_=dr["wk"][g].rearrange("(c p) m -> p c m", p=128)), [("ld", 16)])
    P_.wait("act", "ld", P_.c["ld"])
    split_pair(P_, T, st, wdst3, wkPh, wkPl, ("wk", which, gp_i))
    P_.wait("sp", "act", P_.c["act"])
    P_.wait("sp", "dve", P_.c["dve"])
    P_.do("sp", lambda e, gp_i=gp_i: e.dma_start(
        out=wdst3,
        in_=dr["wv"][:, 256 * gp_i:256 * (gp_i + 1)].rearrange("(c p) m -> p c m", p=128)),
        [("ld", 16)])
    P_.wait("act", "ld", P_.c["ld"])
    split_pair(P_, T, st, wdst3, wvPh, wvPl, ("wv", which, gp_i))

    P_.do("dve", lambda e: e.memset(T["mb"][0:1, MB_KMAX:MB_KMAX + 2], 0.0), [("dve", 1)])

    for t in range(T256):
        P_.wait("sp", "act", st.get(("split_hi", ("x", which, gp_i, t - 1)), 0))
        P_.wait("sp", "dve", st.get(("split_lo", ("x", which, gp_i, t - 1)), 0))
        if t == 0:
            P_.wait("sp", "act", st[("split_hi", ("wv", which, gp_i))])
            P_.wait("sp", "dve", st[("split_lo", ("wv", which, gp_i))])
        P_.do("sp", lambda e, t=t: e.dma_start(
            out=wdst3,
            in_=xkv[:, 256 * t:256 * (t + 1)].rearrange("(c p) n -> p c n", p=128)),
            [("ld", 16)])
        x_ld = P_.c["ld"]
        if t >= T256 - 2 and gp_i == 0:
            dcol = 256 * (t - (T256 - 2))
            P_.do("sp", lambda e, t=t, dcol=dcol: e.dma_start(
                out=T["aXQ"].rearrange("p (c n) -> p c n", n=512)[:, :, dcol:dcol + 256],
                in_=xkv[:, 256 * t:256 * (t + 1)].rearrange("(c p) n -> p c n", p=128)),
                [("ld", 16)])
            x_ld = P_.c["ld"]
        P_.wait("act", "ld", x_ld)
        P_.wait("act", "pe", st.get(("kv_mm_pe", which, gp_i, t - 1), 0))
        P_.wait("dve", "pe", st.get(("kv_mm_pe", which, gp_i, t - 1), 0))
        split_pair(P_, T, st, wdst3, xPh, xPl, ("x", which, gp_i, t))
        xs_hi = st[("split_hi", ("x", which, gp_i, t))]
        xs_lo = st[("split_lo", ("x", which, gp_i, t))]

        kps = [T["psA"], T["psB"]]
        for gl in range(2):
            idx = 2 * t + gl
            bank = kps[idx % 2]
            P_.wait("pe", "act", max(st.get(("kv_evac", which, gp_i, idx - 2), 0), xs_hi))
            P_.wait("pe", "dve", max(st.get(("kv_evac_lo", which, gp_i, idx - 2), 0), xs_lo))
            first = True
            for (wv_, xv_) in ((wkPh, xPh), (wkPh, xPl), (wkPl, xPh)):
                for c in range(NCH):
                    last = (not first) and False
                    P_.do("pe", lambda e, c=c, gl=gl, bank=bank, wv_=wv_, xv_=xv_: e.matmul(
                        bank[:, 0:256], wv_[:, c, 128 * gl:128 * (gl + 1)], xv_[:, c, :],
                        start=first, stop=False),
                        ())
                    first = False
            P_.do("pe", lambda e, gl=gl, bank=bank: e.matmul(
                bank[:, 0:256], wkPl[:, NCH - 1, 128 * gl:128 * (gl + 1)],
                xPl[:, NCH - 1, :], start=False, stop=True), [("pe", 1)])
            kmm = P_.c["pe"]
            t128a, t128b = 2 * t, 2 * t + 1
            t256 = t
            base = 8192 * gl + 512 * t256
            k16 = T["aKY"][:].bitcast(F16)
            khi = k16[:, base:base + 256]
            klo = k16[:, base + 256:base + 512]
            P_.wait("act", "pe", kmm)
            P_.do("act", lambda e, bank=bank, khi=khi: e.activation(khi, bank[:, 0:256],
                                                                    AF.Copy))
            P_.do("act", lambda e, bank=bank, gl=gl: e.activation(
                T["scr"][:, 256 * gl:256 * (gl + 1)], bank[:, 0:256], AF.Square),
                [("act", 1)])
            st[("kv_evac", which, gp_i, idx)] = P_.c["act"]
            P_.wait("dve", "act", P_.c["act"])
            P_.do("dve", lambda e, bank=bank, khi=khi, klo=klo: e.scalar_tensor_tensor(
                klo, bank[:, 0:256], 1.0, khi, ALU.mult, ALU.subtract), [("dve", 1)])
            st[("kv_evac_lo", which, gp_i, idx)] = P_.c["dve"]

        for sub in range(2):
            t128 = 2 * t + sub
            vbank = [T["psC"], T["psD"]][t128 % 2]
            P_.wait("pe", "dve", st.get(("v_evac_lo", which, gp_i, t128 - 2), 0))
            P_.wait("pe", "act", st.get(("v_evac", which, gp_i, t128 - 2), 0))
            passes = [(xPh, wvPh), (xPl, wvPh), (xPh, wvPl)]
            for pi, (xv_, wv_) in enumerate(passes):
                for c in range(NCH):
                    last = (pi == 2 and c == NCH - 1)
                    P_.do("pe", lambda e, c=c, sub=sub, vbank=vbank, xv_=xv_, wv_=wv_,
                          pi=pi, last=last: e.matmul(
                              vbank[:, 0:256],
                              xv_[:, c, 128 * sub:128 * (sub + 1)], wv_[:, c, :],
                              start=(pi == 0 and c == 0), stop=last),
                        [("pe", 1)] if last else ())
            st[("kv_mm_pe", which, gp_i, t)] = P_.c["pe"]
            vhi, vlo = v16_tile(T, t128)
            P_.wait("act", "pe", P_.c["pe"])
            P_.do("act", lambda e, vbank=vbank, vhi=vhi, t128=t128, voff=voff: e.activation(
                vhi, vbank[:, 0:256], AF.Copy,
                scale=T["valid"][:, voff + t128:voff + t128 + 1]), [("act", 1)])
            st[("v_evac", which, gp_i, t128)] = P_.c["act"]
            P_.wait("dve", "act", P_.c["act"])
            P_.do("dve", lambda e, vbank=vbank, vhi=vhi, vlo=vlo, t128=t128, voff=voff:
                  e.scalar_tensor_tensor(vlo, vbank[:, 0:256],
                                         T["valid"][:, voff + t128:voff + t128 + 1],
                                         vhi, ALU.mult, ALU.subtract), [("dve", 1)])
            st[("v_evac_lo", which, gp_i, t128)] = P_.c["dve"]

        for gl in range(2):
            idx = 2 * t + gl
            sbank = [T["psS1"], T["psS2"]][gl]
            P_.wait("pe", "act", st[("kv_evac", which, gp_i, idx)])
            P_.wait("pe", "dve", st.get(("knorm_dve", which, gp_i, idx - 2), 0))
            P_.do("pe", lambda e, gl=gl, sbank=sbank: e.matmul(
                sbank[0:1, 0:256], T["mb"][:, MB_ONESC:MB_ONESC + 1],
                T["scr"][:, 256 * gl:256 * (gl + 1)], start=True, stop=True), [("pe", 1)])
            P_.wait("dve", "pe", P_.c["pe"])
            P_.do("dve", lambda e, sbank=sbank: e.reduce_max(
                srow(T, SR_KN, 1), sbank[0:1, 0:256], axis=AX.X))
            P_.do("dve", lambda e: e.tensor_copy(T["mb"][0:1, 36:40], T["mb"][0:1, 40:44]))
            P_.do("dve", lambda e, gl=gl: e.tensor_tensor(
                T["mb"][0:1, MB_KMAX + gl:MB_KMAX + gl + 1],
                T["mb"][0:1, MB_KMAX + gl:MB_KMAX + gl + 1],
                srow(T, SR_KN, 1), ALU.max), [("dve", 1)])
            st[("knorm_dve", which, gp_i, idx)] = P_.c["dve"]

        if which == "A":
            st["stats_pump"](10)

    st[("kmax_dve", which, gp_i)] = P_.c["dve"]


def split_xq(P_, T, dr, st, which):
    P_.barrier()
    xqv = T["aXQ"].rearrange("p (c n) -> p c n", n=512)
    xh, xl = xq16(T)
    for c in range(NCH):
        P_.do("dve", lambda e, c=c: e.tensor_copy(T["ctmp"][:], xqv[:, c, :]), [("dve", 1)])
        P_.wait("act", "dve", P_.c["dve"])
        P_.do("act", lambda e, c=c: e.activation(xh[:, c, :], T["ctmp"][:], AF.Copy),
              [("act", 1)])
        P_.wait("dve", "act", P_.c["act"])
        P_.do("dve", lambda e, c=c: e.scalar_tensor_tensor(
            xl[:, c, :], T["ctmp"][:], 1.0, xh[:, c, :], ALU.mult, ALU.subtract),
            [("dve", 1)])
    st[("xqsplit_dve", which)] = P_.c["dve"]
    st[("xqsplit_act", which)] = P_.c["act"]


def attn(P_, T, dr, st, which, gp_i, TT, voff):
    P_.barrier()
    str16 = T["aSTR"][:].bitcast(F16)
    wqD = T["aSTR"][:, 0:2048]
    wqD3 = wqD.rearrange("p (c n) -> p c n", n=128)
    wqP_sl = []
    for k in range(2):
        base = T["aSTR"][:, 2048 * (k + 1):2048 * (k + 2)].bitcast(F16)
        v = base.rearrange("p (c n) -> p c n", n=256)
        wqP_sl.append((v[:, :, 0:128], v[:, :, 128:256]))
    qh_sl = [(str16[:, 12288:12800], str16[:, 12800:13312]),
             (str16[:, 13312:13824], str16[:, 13824:14336])]
    pt32 = [T["aSTR"][:, 7168 + 512 * k:7680 + 512 * k] for k in range(4)]
    ptP = [(str16[:, 18432 + 1024 * k:18944 + 1024 * k],
            str16[:, 18944 + 1024 * k:19456 + 1024 * k]) for k in range(4)]
    xh, xl = xq16(T)
    ch_, cl_ = ctx16(T)

    def wq_dma(j):
        if j >= 8:
            return
        H = 8 * gp_i + j
        P_.wait("sp", "act", st.get(("split_hi", ("wq", which, gp_i, j - 1)), 0))
        P_.wait("sp", "dve", st.get(("split_lo", ("wq", which, gp_i, j - 1)), 0))
        P_.do("sp", lambda e, H=H: e.dma_start(
            out=wqD3, in_=dr["wq"][H].rearrange("(c p) m -> p c m", p=128)), [("ld", 16)])
        st[("wq_ld", which, gp_i, j)] = P_.c["ld"]

    def wq_split(j):
        if j >= 8:
            return
        P_.wait("act", "ld", st[("wq_ld", which, gp_i, j)])
        P_.wait("act", "pe", st.get(("qproj_pe", which, 8 * gp_i + j - 1), 0))
        P_.wait("dve", "pe", st.get(("qproj_pe", which, 8 * gp_i + j - 1), 0))
        split_pair(P_, T, st, wqD3, wqP_sl[j % 2][0], wqP_sl[j % 2][1],
                   ("wq", which, gp_i, j))

    def wq_fetch(j):
        wq_dma(j)
        wq_split(j)

    def head_front(j):
        if j >= 8:
            return
        H = 8 * gp_i + j
        gl = (H // 4) - 2 * gp_i
        hb = j % 2
        qbank = [T["psE"], T["psF"]][hb]
        wqPh, wqPl = wqP_sl[j % 2]
        negm_col = [MB_NEGM, MB_NEGM2][hb]
        qhh, qhl = qh_sl[hb]

        P_.wait("pe", "act", st[("split_hi", ("wq", which, gp_i, j))])
        P_.wait("pe", "dve", st[("split_lo", ("wq", which, gp_i, j))])
        if j >= 2:
            P_.wait("pe", "act", st[("q_evac_act", which, H - 2)])
            P_.wait("pe", "dve", st[("ctx_dve", which, H - 2)])
        for pi, (wv_, xv_) in enumerate([(wqPh, xh), (wqPh, xl), (wqPl, xh)]):
            for c in range(NCH):
                last = (pi == 2 and c == NCH - 1)
                P_.do("pe", lambda e, c=c, qbank=qbank, wv_=wv_, xv_=xv_, pi=pi,
                      last=last: e.matmul(
                    qbank[:], wv_[:, c, :], xv_[:, c, :],
                    start=(pi == 0 and c == 0), stop=last), [("pe", 1)] if last else ())
        st[("qproj_pe", which, H)] = P_.c["pe"]

        P_.wait("act", "pe", P_.c["pe"])
        P_.do("act", lambda e, qhh=qhh, qbank=qbank: e.activation(qhh, qbank[:], AF.Copy))
        P_.do("act", lambda e, qbank=qbank: e.activation(T["scr"][:], qbank[:], AF.Square),
              [("act", 1)])
        st[("q_evac_act", which, H)] = P_.c["act"]
        P_.wait("dve", "act", P_.c["act"])
        P_.do("dve", lambda e, qhh=qhh, qhl=qhl, qbank=qbank: e.scalar_tensor_tensor(
            qhl, qbank[:], 1.0, qhh, ALU.mult, ALU.subtract), [("dve", 1)])
        st[("q_lo_dve", which, H)] = P_.c["dve"]
        P_.wait("pe", "act", st[("q_evac_act", which, H)])
        P_.wait("pe", "dve", P_.c["dve"])
        P_.do("pe", lambda e, qbank=qbank: e.matmul(
            qbank[0:1, 0:512], T["mb"][:, MB_ONESC:MB_ONESC + 1], T["scr"][:],
            start=True, stop=True), [("pe", 1)])
        P_.wait("dve", "pe", P_.c["pe"])
        P_.do("dve", lambda e, qbank=qbank: e.reduce_max(srow(T, SR_KN, 1),
                                                         qbank[0:1, 0:512], axis=AX.X))
        P_.do("dve", lambda e: e.tensor_copy(T["mb"][0:1, 36:40], T["mb"][0:1, 40:44]))
        P_.do("dve", lambda e, gl=gl: e.tensor_tensor(
            T["rowD"][0:1, 34:35], srow(T, SR_KN, 1),
            T["mb"][0:1, MB_KMAX + gl:MB_KMAX + gl + 1], ALU.mult), [("dve", 1)])
        P_.wait("act", "dve", P_.c["dve"])
        P_.do("act", lambda e: e.activation(T["rowD"][0:1, 32:33], T["rowD"][0:1, 34:35],
                                            AF.Sqrt, scale=1.0 / 128.0), [("act", 1)])
        P_.wait("dve", "act", P_.c["act"])
        P_.do("dve", lambda e: e.tensor_scalar(T["rowD"][0:1, 32:33], T["rowD"][0:1, 32:33],
                                               -1.0, None, ALU.mult), [("dve", 1)])
        P_.wait("pe", "dve", P_.c["dve"])
        P_.do("pe", lambda e, qbank=qbank: e.matmul(
            qbank[:, 0:1], T["ones1"][:], T["rowD"][0:1, 32:33],
            start=True, stop=True), [("pe", 1)])
        P_.wait("dve", "pe", P_.c["pe"])
        P_.do("dve", lambda e, qbank=qbank, negm_col=negm_col: e.tensor_copy(
            T["mb"][:, negm_col:negm_col + 1], qbank[:, 0:1]), [("dve", 1)])
        st[("negm_dve", which, H)] = P_.c["dve"]

    wq_fetch(0)
    head_front(0)
    for j in range(8):
        H = 8 * gp_i + j
        gl = (H // 4) - 2 * gp_i
        hb = j % 2
        qbank = [T["psE"], T["psF"]][hb]
        ctxps = [T["psC"], T["psD"]][hb]
        sps = [T["psS1"], T["psS2"]][hb]
        qhh, qhl = qh_sl[hb]
        negm_col = [MB_NEGM, MB_NEGM2][hb]
        negm_dve = st[("negm_dve", which, H)]
        wq_dma(j + 1)

        stps = [T["psA"], T["psB"]]

        def emit_S(t):
            diag_j = t - (TT - 4)
            qlo = 128 * diag_j if diag_j >= 0 else 0
            bank = stps[t % 2]
            P_.wait("pe", "act", st.get(("exp_act", which, H, t - 2), 0))
            khi, klo = k16_tile(T, gl, t)
            first = True
            for (kv_, qv_) in ((khi, qhh), (khi, qhl), (klo, qhh)):
                last = (kv_ is klo)
                P_.do("pe", lambda e, t=t, qlo=qlo, bank=bank, kv_=kv_, qv_=qv_,
                      first=first, last=last: e.matmul(
                          bank[:, qlo:512], kv_, qv_[:, qlo:512],
                          start=first, stop=last), [("pe", 1)] if last else ())
                first = False
            st[("st_pe", which, H, t)] = P_.c["pe"]

        def qlo_of(t):
            diag_j = t - (TT - 4)
            return 128 * diag_j if diag_j >= 0 else 0

        def emit_lo(t):
            qlo = qlo_of(t)
            w = 512 - qlo
            pth, ptl = ptP[t % 4]
            P_.wait("dve", "act", st[("pth_act", which, H, t)])
            P_.do("dve", lambda e, t=t, w=w, pth=pth, ptl=ptl, voff=voff:
                  e.scalar_tensor_tensor(ptl[:, 0:w], pt32[t % 4][:, 0:w],
                                         T["validsc"][:, voff + t:voff + t + 1],
                                         pth[:, 0:w], ALU.mult, ALU.subtract),
                  [("dve", 1)])
            st[("ptl_dve", which, H, t)] = P_.c["dve"]
            P_.wait("pe", "dve", P_.c["dve"])
            vhi, vlo = v16_tile(T, t)
            P_.do("pe", lambda e, t=t, qlo=qlo, w=w, sps=sps, ptl=ptl: e.matmul(
                sps[0:1, qlo:512], T["ones16"][:], ptl[:, 0:w],
                start=False, stop=(t == TT - 1)))
            P_.do("pe", lambda e, t=t, qlo=qlo, w=w, gl=gl, ctxps=ctxps, vhi=vhi,
                  ptl=ptl: e.matmul(
                      ctxps[:, qlo:512], vhi[:, 128 * gl:128 * (gl + 1)], ptl[:, 0:w],
                      start=False, stop=(t == TT - 1)), [("pe", 1)])
            st[("lo_pe", which, H, t)] = P_.c["pe"]

        emit_S(0)
        for t in range(TT):
            if t + 1 < TT:
                emit_S(t + 1)
            if t == 2:
                wq_split(j + 1)
            if t == TT // 2:
                head_front(j + 1)
            diag_j = t - (TT - 4)
            qlo = qlo_of(t)
            w = 512 - qlo
            bank = stps[t % 2]
            if diag_j >= 0:
                P_.wait("dve", "pe", st[("st_pe", which, H, t)])
                P_.do("dve", lambda e, bank=bank, qlo=qlo: e.tensor_tensor(
                    bank[:, qlo:qlo + 128], bank[:, qlo:qlo + 128], T["tri"][:], ALU.add),
                    [("dve", 1)])
                st[("tri_dve", which, H, t)] = P_.c["dve"]
            P_.wait("act", "pe", st[("st_pe", which, H, t)])
            if diag_j >= 0:
                P_.wait("act", "dve", st[("tri_dve", which, H, t)])
            if t == 0:
                P_.wait("act", "dve", negm_dve)
            P_.wait("act", "dve", st.get(("ptl_dve", which, H, t - 4), 0))
            P_.do("act", lambda e, t=t, bank=bank, qlo=qlo, w=w, negm_col=negm_col:
                  e.activation(
                      pt32[t % 4][:, 0:w], bank[:, qlo:512], AF.Exp,
                      bias=T["mb"][:, negm_col:negm_col + 1], scale=SCALE), [("act", 1)])
            st[("exp_act", which, H, t)] = P_.c["act"]
            P_.wait("act", "pe", st.get(("lo_pe", which, H, t - 4), 0))
            pth, ptl = ptP[t % 4]
            P_.do("act", lambda e, t=t, w=w, pth=pth, voff=voff: e.activation(
                pth[:, 0:w], pt32[t % 4][:, 0:w], AF.Copy,
                scale=T["validsc"][:, voff + t:voff + t + 1]), [("act", 1)])
            st[("pth_act", which, H, t)] = P_.c["act"]
            P_.wait("pe", "act", P_.c["act"])
            vhi, vlo = v16_tile(T, t)
            P_.do("pe", lambda e, t=t, qlo=qlo, w=w, sps=sps, pth=pth: e.matmul(
                sps[0:1, qlo:512], T["ones16"][:], pth[:, 0:w],
                start=(t == 0), stop=False))
            P_.do("pe", lambda e, t=t, qlo=qlo, w=w, gl=gl, ctxps=ctxps, vhi=vhi,
                  pth=pth: e.matmul(
                      ctxps[:, qlo:512], vhi[:, 128 * gl:128 * (gl + 1)], pth[:, 0:w],
                      start=(t == 0), stop=False))
            P_.do("pe", lambda e, t=t, qlo=qlo, w=w, gl=gl, ctxps=ctxps, vlo=vlo,
                  pth=pth: e.matmul(
                      ctxps[:, qlo:512], vlo[:, 128 * gl:128 * (gl + 1)], pth[:, 0:w],
                      start=False, stop=False), [("pe", 1)])
            st[("av_pe", which, H, t)] = P_.c["pe"]
            if t >= 2:
                emit_lo(t - 2)
        emit_lo(TT - 2)
        emit_lo(TT - 1)

        P_.wait("dve", "pe", P_.c["pe"])
        P_.do("dve", lambda e, sps=sps: e.reciprocal(T["rowA"][:], sps[0:1, 0:512]),
              [("dve", 1)])
        P_.wait("pe", "dve", P_.c["dve"])
        P_.do("pe", lambda e, qbank=qbank: e.matmul(
            qbank[:], T["ones1"][:], T["rowA"][:], start=True, stop=True), [("pe", 1)])
        P_.wait("dve", "pe", P_.c["pe"])
        P_.do("dve", lambda e, qbank=qbank: e.tensor_copy(T["rsb"][:], qbank[:]))
        P_.do("dve", lambda e, H=H, ctxps=ctxps: e.tensor_tensor(
            T["ctmp"][:], ctxps[:], T["rsb"][:], ALU.mult), [("dve", 1)])
        P_.wait("act", "dve", P_.c["dve"])
        P_.do("act", lambda e, H=H, ch_=ch_: e.activation(ch_[:, H, :], T["ctmp"][:],
                                                          AF.Copy), [("act", 1)])
        P_.wait("dve", "act", P_.c["act"])
        P_.do("dve", lambda e, H=H, ch_=ch_, cl_=cl_: e.scalar_tensor_tensor(
            cl_[:, H, :], T["ctmp"][:], 1.0, ch_[:, H, :], ALU.mult, ALU.subtract),
            [("dve", 1)])
        st[("ctx_dve", which, H)] = P_.c["dve"]
        if which == "A":
            st["stats_pump"](10)


def wo_ln1(P_, T, dr, st, which):
    P_.barrier()
    str16 = T["aSTR"][:].bitcast(F16)
    woD = T["aSTR"][:, 0:2048]
    woD3 = woD.rearrange("p (c n) -> p c n", n=128)
    woPh = str16[:, 4096:8192].rearrange("p (c n) -> p c n", n=256)[:, :, 0:128]
    woPl = str16[:, 4096:8192].rearrange("p (c n) -> p c n", n=256)[:, :, 128:256]
    v32 = T["aKY"].rearrange("p (c n) -> p c n", n=512)
    ch_, cl_ = ctx16(T)
    xh, xl = xq16(T)

    for do in range(NCH):
        P_.wait("sp", "act", st.get(("split_hi", ("wo", which, do - 1)), 0))
        P_.wait("sp", "dve", st.get(("split_lo", ("wo", which, do - 1)), 0))
        P_.do("sp", lambda e, do=do: e.dma_start(
            out=woD3, in_=dr["wo"][do].rearrange("(c p) m -> p c m", p=128)), [("ld", 16)])
        P_.wait("act", "ld", P_.c["ld"])
        P_.wait("act", "pe", st.get(("wo_pe", which, do - 1), 0))
        P_.wait("dve", "pe", st.get(("wo_pe", which, do - 1), 0))
        split_pair(P_, T, st, woD3, woPh, woPl, ("wo", which, do))
        P_.wait("pe", "act", st[("split_hi", ("wo", which, do))])
        P_.wait("pe", "dve", st[("split_lo", ("wo", which, do))])
        bank = [T["psA"], T["psB"]][do % 2]
        P_.wait("pe", "dve", st.get(("wo_evac", which, do - 2), 0))
        for pi, (wv_, cv_) in enumerate([(woPh, ch_), (woPh, cl_), (woPl, ch_)]):
            for c in range(NCH):
                last = (pi == 2 and c == NCH - 1)
                P_.do("pe", lambda e, c=c, bank=bank, wv_=wv_, cv_=cv_, pi=pi,
                      last=last: e.matmul(
                    bank[:], wv_[:, c, :], cv_[:, c, :],
                    start=(pi == 0 and c == 0), stop=last), [("pe", 1)] if last else ())
        st[("wo_pe", which, do)] = P_.c["pe"]
        P_.wait("dve", "pe", P_.c["pe"])
        P_.do("dve", lambda e, do=do, bank=bank, v32=v32: e.scalar_tensor_tensor(
            v32[:, do, :], bank[:], T["bo"][:, do:do + 1], xh[:, do, :],
            ALU.add, ALU.add))
        P_.do("dve", lambda e, do=do, v32=v32: e.tensor_tensor(
            v32[:, do, :], v32[:, do, :], xl[:, do, :], ALU.add), [("dve", 1)])
        st[("wo_evac", which, do)] = P_.c["dve"]
        P_.wait("pe", "dve", P_.c["dve"])
        P_.do("pe", lambda e, do=do, v32=v32: e.matmul(
            T["psS1"][0:1, 0:512], T["mb"][:, MB_ONESC:MB_ONESC + 1], v32[:, do, :],
            start=(do == 0), stop=(do == NCH - 1)), [("pe", 1)])
        P_.wait("act", "pe", P_.c["pe"])
        P_.wait("act", "pe", st.get(("sq_mm_pe", which, do - 1), 0))
        P_.do("act", lambda e, do=do, v32=v32: e.activation(T["scr"][:], v32[:, do, :],
                                                            AF.Square), [("act", 1)])
        P_.wait("pe", "act", P_.c["act"])
        P_.do("pe", lambda e, do=do: e.matmul(
            T["psS2"][0:1, 0:512], T["mb"][:, MB_ONESC:MB_ONESC + 1], T["scr"][:],
            start=(do == 0), stop=(do == NCH - 1)), [("pe", 1)])
        st[("sq_mm_pe", which, do)] = P_.c["pe"]

    ky16 = T["aKY"][:].bitcast(F16).rearrange("p (c n) -> p c n", n=1024)
    y1h = ky16[:, :, 0:512]
    y1l = ky16[:, :, 512:1024]
    ln_norm_pair(P_, T, st, v32, (y1h, y1l), which + "ln1")


def ln_norm(P_, T, st, src_v, dst_v, tag, pair=None):
    P_.wait("dve", "pe", P_.c["pe"])
    P_.do("dve", lambda e: e.tensor_scalar(T["rowB"][:], T["psS1"][0:1, 0:512],
                                           1.0 / D, None, ALU.mult))
    P_.do("dve", lambda e: e.tensor_scalar(T["rowA"][:], T["psS2"][0:1, 0:512],
                                           1.0 / D, None, ALU.mult))
    P_.do("dve", lambda e: e.tensor_tensor(T["rowE"][:], T["rowB"][:],
                                           T["rowB"][:], ALU.mult))
    P_.do("dve", lambda e: e.tensor_tensor(T["rowA"][:], T["rowA"][:],
                                           T["rowE"][:], ALU.subtract), [("dve", 1)])
    P_.wait("act", "dve", P_.c["dve"])
    P_.do("act", lambda e: e.activation(T["rowA"][:], T["rowA"][:], AF.Sqrt,
                                        bias=T["rowD"][0:1, 48:49]), [("act", 1)])
    P_.wait("dve", "act", P_.c["act"])
    P_.do("dve", lambda e: e.reciprocal(T["rowC"][:], T["rowA"][:]), [("dve", 1)])
    P_.wait("pe", "dve", P_.c["dve"])
    P_.do("pe", lambda e: e.matmul(T["psE"][:], T["ones1"][:], T["rowB"][:],
                                   start=True, stop=True))
    P_.do("pe", lambda e: e.matmul(T["psF"][:], T["ones1"][:], T["rowC"][:],
                                   start=True, stop=True), [("pe", 1)])
    P_.wait("dve", "pe", P_.c["pe"])
    for do in range(NCH):
        P_.do("dve", lambda e, do=do: e.tensor_tensor(
            dst_v[:, do, :], src_v[:, do, :], T["psE"][:], ALU.subtract))
        P_.do("dve", lambda e, do=do: e.tensor_tensor(
            dst_v[:, do, :], dst_v[:, do, :], T["psF"][:], ALU.mult), [("dve", 1)])
        if pair is not None:
            hi, lo = pair
            P_.do("dve", lambda e, do=do, hi=hi: e.tensor_copy(hi[:, do, :],
                                                               dst_v[:, do, :]))
            P_.do("dve", lambda e, do=do, hi=hi, lo=lo: e.scalar_tensor_tensor(
                lo[:, do, :], dst_v[:, do, :], 1.0, hi[:, do, :],
                ALU.mult, ALU.subtract), [("dve", 1)])
    st[("ln_done_dve", tag)] = P_.c["dve"]


def ln_norm_pair(P_, T, st, src_v, pair, tag):
    hi, lo = pair
    P_.wait("dve", "pe", P_.c["pe"])
    P_.do("dve", lambda e: e.tensor_scalar(T["rowB"][:], T["psS1"][0:1, 0:512],
                                           1.0 / D, None, ALU.mult))
    P_.do("dve", lambda e: e.tensor_scalar(T["rowA"][:], T["psS2"][0:1, 0:512],
                                           1.0 / D, None, ALU.mult))
    P_.do("dve", lambda e: e.tensor_tensor(T["rowE"][:], T["rowB"][:],
                                           T["rowB"][:], ALU.mult))
    P_.do("dve", lambda e: e.tensor_tensor(T["rowA"][:], T["rowA"][:],
                                           T["rowE"][:], ALU.subtract), [("dve", 1)])
    P_.wait("act", "dve", P_.c["dve"])
    P_.do("act", lambda e: e.activation(T["rowA"][:], T["rowA"][:], AF.Sqrt,
                                        bias=T["rowD"][0:1, 48:49]), [("act", 1)])
    P_.wait("dve", "act", P_.c["act"])
    P_.do("dve", lambda e: e.reciprocal(T["rowC"][:], T["rowA"][:]), [("dve", 1)])
    P_.wait("pe", "dve", P_.c["dve"])
    P_.do("pe", lambda e: e.matmul(T["psE"][:], T["ones1"][:], T["rowB"][:],
                                   start=True, stop=True))
    P_.do("pe", lambda e: e.matmul(T["psF"][:], T["ones1"][:], T["rowC"][:],
                                   start=True, stop=True), [("pe", 1)])
    P_.wait("dve", "pe", P_.c["pe"])
    tmp = T["rsb"]
    for do in range(NCH):
        P_.do("dve", lambda e, do=do, tmp=tmp: e.tensor_tensor(
            tmp[:], src_v[:, do, :], T["psE"][:], ALU.subtract))
        P_.do("dve", lambda e, do=do, tmp=tmp: e.tensor_tensor(
            tmp[:], tmp[:], T["psF"][:], ALU.mult))
        P_.do("dve", lambda e, do=do, hi=hi, tmp=tmp: e.tensor_copy(hi[:, do, :], tmp[:]))
        P_.do("dve", lambda e, do=do, hi=hi, lo=lo, tmp=tmp: e.scalar_tensor_tensor(
            lo[:, do, :], tmp[:], 1.0, hi[:, do, :], ALU.mult, ALU.subtract),
            [("dve", 1)])
    st[("ln_done_dve", tag)] = P_.c["dve"]


def ffn(P_, T, dr, st, which):
    P_.barrier()
    ky16 = T["aKY"][:].bitcast(F16).rearrange("p (c n) -> p c n", n=1024)
    y1h = ky16[:, :, 0:512]
    y1l = ky16[:, :, 512:1024]
    v2 = T["aXQ"].rearrange("p (c n) -> p c n", n=512)
    v16 = T["aV"][:].bitcast(F16).rearrange("p (c n) -> p c n", n=1024)
    h2h = v16[:, :, 0:512]
    h2l = v16[:, :, 512:1024]
    wfcb = [T["aSTR"][:, 2048 * k:2048 * (k + 1)] for k in range(4)]
    qfc_sl = [T["aSTR"][:, 8192 + 1024 * k:9216 + 1024 * k].bitcast(F16) for k in range(2)]
    wprb = [T["aCTX"][:, 1280 * k:1280 * (k + 1)] for k in range(4)]
    qpr_sl = [T["aCTX"][:, 5120 + 640 * k:5760 + 640 * k].bitcast(F16) for k in range(2)]
    scr2 = T["aCTX"][:, 6400:6912]
    cdve = st["ffn_consts_dve"]

    for q in range(NQUARTER):
        for hc in range(HH_PER_Q):
            hh = q * HH_PER_Q + hc
            P_.wait("sp", "dve", st.get(("tern_fc", which, hh - 4), 0))
            P_.do("sp", lambda e, hh=hh: e.dma_start(
                out=wfcb[hh % 4].rearrange("p (c n) -> p c n", n=128),
                in_=dr["wfc"][hh].rearrange("(c p) m -> p c m", p=128)), [("ld", 16)])
            qfc = qfc_sl[hh % 2]
            P_.wait("dve", "ld", P_.c["ld"])
            P_.wait("dve", "pe", st.get(("mm1_pe", which, hh - 2), 0))
            P_.do("dve", lambda e, qfc=qfc, hh=hh: e.tensor_scalar(
                qfc[:], wfcb[hh % 4][:], T["mb"][:, MB_NTHRF:MB_NTHRF + 1], None, ALU.is_lt))
            P_.do("dve", lambda e, qfc=qfc, hh=hh: e.scalar_tensor_tensor(
                qfc[:], wfcb[hh % 4][:], T["mb"][:, MB_THRF:MB_THRF + 1], qfc[:],
                ALU.is_gt, ALU.subtract), [("dve", 1)])
            st[("tern_fc", which, hh)] = P_.c["dve"]
            P_.wait("pe", "dve", P_.c["dve"])
            bank = [T["psA"], T["psB"]][hc % 2]
            P_.wait("pe", "act", st.get(("h_evac", which, q, hc - 2), 0))
            qv = qfc.rearrange("p (c n) -> p c n", n=128)
            for half in range(2):
                yv = y1h if half == 0 else y1l
                for c in range(NCH):
                    P_.do("pe", lambda e, c=c, bank=bank, qv=qv, yv=yv: e.matmul(
                        bank[:], qv[:, c, :], yv[:, c, :],
                        start=(half == 0 and c == 0), stop=(half == 1 and c == NCH - 1)),
                        [("pe", 1)] if (half == 1 and c == NCH - 1) else ())
            st[("mm1_pe", which, hh)] = P_.c["pe"]
            P_.wait("act", "pe", P_.c["pe"])
            if hc >= 1 or q >= 1:
                P_.wait("act", "dve", st.get(("h2l_dve", which, q * HH_PER_Q + hc - 1), 0))
            P_.do("act", lambda e, bank=bank: e.activation(T["scr"][:], bank[:], AF.Relu))
            P_.do("act", lambda e, scr2=scr2: e.activation(scr2[:], T["scr"][:], AF.Square,
                                                           scale=0.0625))
            P_.do("act", lambda e, hc=hc, scr2=scr2: e.activation(h2h[:, hc, :], scr2[:],
                                                                  AF.Copy), [("act", 1)])
            st[("h_evac", which, q, hc)] = P_.c["act"]
            P_.wait("dve", "act", P_.c["act"])
            P_.do("dve", lambda e, hc=hc, scr2=scr2: e.scalar_tensor_tensor(
                h2l[:, hc, :], scr2[:], 1.0, h2h[:, hc, :], ALU.mult, ALU.subtract),
                [("dve", 1)])
            st[("h2l_dve", which, hh)] = P_.c["dve"]

        for do in range(NCH):
            dd = q * NCH + do
            P_.wait("sp", "dve", st.get(("tern_pr", which, dd - 4), 0))
            P_.do("sp", lambda e, do=do, q=q, dd=dd: e.dma_start(
                out=wprb[dd % 4].rearrange("p (c n) -> p c n", n=128),
                in_=dr["wpr"][do][1280 * q:1280 * (q + 1), :].rearrange(
                    "(c p) m -> p c m", p=128)), [("ld", 16)])
            P_.wait("dve", "ld", P_.c["ld"])
            if do == 0:
                P_.wait("dve", "dve", st[("h2l_dve", which, q * HH_PER_Q + HH_PER_Q - 1)])
                P_.wait("dve", "act", st[("h_evac", which, q, HH_PER_Q - 1)])
            qpr = qpr_sl[dd % 2]
            P_.wait("dve", "pe", st.get(("mm2_pe", which, dd - 2), 0))
            P_.do("dve", lambda e, qpr=qpr, dd=dd: e.tensor_scalar(
                qpr[:], wprb[dd % 4][:], T["mb"][:, MB_NTHRP:MB_NTHRP + 1], None, ALU.is_lt))
            P_.do("dve", lambda e, qpr=qpr, dd=dd: e.scalar_tensor_tensor(
                qpr[:], wprb[dd % 4][:], T["mb"][:, MB_THRP:MB_THRP + 1], qpr[:],
                ALU.is_gt, ALU.subtract), [("dve", 1)])
            st[("tern_pr", which, dd)] = P_.c["dve"]
            P_.wait("pe", "dve", P_.c["dve"])
            bank = [T["psC"], T["psD"]][dd % 2]
            P_.wait("pe", "dve", st.get(("v2_evac", which, dd - 2), 0))
            qprv = qpr.rearrange("p (c n) -> p c n", n=128)
            for half in range(2):
                hv = h2h if half == 0 else h2l
                for hc in range(HH_PER_Q):
                    P_.do("pe", lambda e, hc=hc, bank=bank, qprv=qprv, hv=hv: e.matmul(
                        bank[:], qprv[:, hc, :], hv[:, hc, :],
                        start=(half == 0 and hc == 0),
                        stop=(half == 1 and hc == HH_PER_Q - 1)),
                        [("pe", 1)] if (half == 1 and hc == HH_PER_Q - 1) else ())
            st[("mm2_pe", which, dd)] = P_.c["pe"]
            P_.wait("dve", "pe", P_.c["pe"])
            if q == 0:
                P_.do("dve", lambda e, do=do, bank=bank, v2=v2: e.scalar_tensor_tensor(
                    v2[:, do, :], bank[:], T["mb"][:, MB_CC:MB_CC + 1], y1h[:, do, :],
                    ALU.mult, ALU.add))
                P_.do("dve", lambda e, do=do, v2=v2: e.tensor_tensor(
                    v2[:, do, :], v2[:, do, :], y1l[:, do, :], ALU.add), [("dve", 1)])
            else:
                P_.do("dve", lambda e, do=do, bank=bank, v2=v2: e.scalar_tensor_tensor(
                    v2[:, do, :], bank[:], T["mb"][:, MB_CC:MB_CC + 1], v2[:, do, :],
                    ALU.mult, ALU.add), [("dve", 1)])
            st[("v2_evac", which, dd)] = P_.c["dve"]


def ln2_out(P_, T, dr, st, which, out_col):
    P_.barrier()
    v2 = T["aXQ"].rearrange("p (c n) -> p c n", n=512)
    for do in range(NCH):
        P_.do("pe", lambda e, do=do, v2=v2: e.matmul(
            T["psS1"][0:1, 0:512], T["mb"][:, MB_ONESC:MB_ONESC + 1], v2[:, do, :],
            start=(do == 0), stop=(do == NCH - 1)), [("pe", 1)])
        P_.wait("act", "pe", P_.c["pe"])
        P_.wait("act", "pe", st.get(("sq2_mm_pe", which, do - 1), 0))
        P_.do("act", lambda e, do=do, v2=v2: e.activation(T["scr"][:], v2[:, do, :],
                                                          AF.Square), [("act", 1)])
        P_.wait("pe", "act", P_.c["act"])
        P_.do("pe", lambda e, do=do: e.matmul(
            T["psS2"][0:1, 0:512], T["mb"][:, MB_ONESC:MB_ONESC + 1], T["scr"][:],
            start=(do == 0), stop=(do == NCH - 1)), [("pe", 1)])
        st[("sq2_mm_pe", which, do)] = P_.c["pe"]

    ln_norm(P_, T, st, v2, v2, which + "ln2")
    P_.wait("act", "dve", P_.c["dve"])
    for do in range(NCH):
        P_.do("act", lambda e, do=do, v2=v2: e.dma_start(
            out=dr["y2"][128 * do:128 * (do + 1), out_col:out_col + CH],
            in_=v2[:, do, :]), [("so", 16)])
    P_.wait("sp", "so", P_.c["so"])



def kernel(L0_attn_x, L0_attn_Wq, L0_attn_Wk, L0_attn_Wv, L0_attn_Wo, L0_attn_bo,
           L0_ffn_w_fc, L0_ffn_w_proj):
    x = np.asarray(L0_attn_x, dtype=np.float32)
    Wq = np.asarray(L0_attn_Wq, dtype=np.float32)
    Wk = np.asarray(L0_attn_Wk, dtype=np.float32)
    Wv = np.asarray(L0_attn_Wv, dtype=np.float32)
    Wo = np.asarray(L0_attn_Wo, dtype=np.float32)
    bo = np.asarray(L0_attn_bo, dtype=np.float32)
    Wfc = np.asarray(L0_ffn_w_fc, dtype=np.float32)
    Wpr = np.asarray(L0_ffn_w_proj, dtype=np.float32)

    wq_blk = np.ascontiguousarray(Wq.reshape(D, HEADS, 128).transpose(1, 0, 2))
    wk_blk = np.ascontiguousarray(Wk.reshape(D, NGRP, 128).transpose(1, 0, 2))
    wo_blk = np.ascontiguousarray(Wo.reshape(D, NCH, 128).transpose(1, 0, 2))
    bo_t = np.ascontiguousarray(bo.reshape(NCH, 128).T)
    wfc_blk = np.ascontiguousarray(Wfc.T.reshape(D, NHH, 128).transpose(1, 0, 2))
    wpr_blk = np.ascontiguousarray(Wpr.T.reshape(HFF, NCH, 128).transpose(1, 0, 2))
    tri = np.where(np.arange(128)[:, None] <= np.arange(128)[None, :],
                   np.float32(0.0), np.float32(NEG_BIG)).astype(np.float32)

    in_maps = []
    for core in range(8):
        b, i = divmod(core, 4)
        xt = np.ascontiguousarray(x[b].T)
        cA, cB = i, 4 + i
        pA, pB = CH * cA, CH * cB

        xkvA = np.zeros((D, PA), dtype=np.float32)
        xkvA[:, :pA] = xt[:, :pA]
        xkvA[:, PA - CH:] = xt[:, CH * cA:CH * (cA + 1)]
        xkvB = np.zeros((D, PB), dtype=np.float32)
        xkvB[:, :pB] = xt[:, :pB]
        xkvB[:, PB - CH:] = xt[:, CH * cB:CH * (cB + 1)]

        def mk_valid(Pn, p_real):
            idx = np.arange(Pn)
            v = ((idx < p_real) | (idx >= Pn - CH)).astype(np.float32)
            return np.ascontiguousarray(v.reshape(Pn // 128, 128).T)

        in_maps.append({
            "xkvA": xkvA, "xkvB": xkvB,
            "validA": mk_valid(PA, pA), "validB": mk_valid(PB, pB),
            "tri": tri,
            "wq": wq_blk, "wk": wk_blk, "wv": Wv, "wo": wo_blk, "bo": bo_t,
            "wfc": wfc_blk, "wpr": wpr_blk,
        })

    nc = build_program()
    res = run_bass_kernel_spmd(nc, in_maps, core_ids=list(range(8)))

    yt = np.empty((2, D, SEQ), dtype=np.float32)
    for core in range(8):
        b, i = divmod(core, 4)
        o = res.results[core]["y2"]
        yt[b][:, CH * i:CH * (i + 1)] = o[:, 0:CH]
        yt[b][:, CH * (4 + i):CH * (5 + i)] = o[:, CH:2 * CH]
    return np.ascontiguousarray(yt.transpose(0, 2, 1))


# version 17
# speedup vs baseline: 1.3419x; 1.3088x over previous
import numpy as np
import concourse.bass as bass
import concourse.mybir as mybir
from concourse.bass_utils import run_bass_kernel_spmd

F32 = mybir.dt.float32
F16 = mybir.dt.float16
AF = mybir.ActivationFunctionType
ALU = mybir.AluOpType
AX = mybir.AxisListType

D = 2048
NCH = 16
HEADS = 16
NGRP = 4
SEQ = 4096
CH = 512
PA, PB = 2048, 4096
TA, TB = PA // 128, PB // 128
HFF = 5120
NHH = 40
NQUARTER = 4
HH_PER_Q = 10
SCALE = float(1.0 / np.sqrt(np.float32(128.0)))
NEG_BIG = -1.0e30
EPS = 1e-5

STAT_CHUNK = 512
N_STAT_FC = (NHH * D * 128) // (128 * STAT_CHUNK)
N_STAT_PR = (NCH * HFF * 128) // (128 * STAT_CHUNK)
N_STAT = N_STAT_FC + N_STAT_PR

MB_ONESC = 0
MB_NEGM = 1
MB_THRF = 2
MB_NTHRF = 3
MB_THRP = 4
MB_NTHRP = 5
MB_CC = 6
MB_RED0 = 8
MB_KMAX = 10

SR_KN = 0
SR_RS = 1
SR_MEAN = 2
SR_RSTD = 3
SR_TMP = 4
SR_TMP2 = 5
SR_M11 = 6
SR_NEGMROW = 7


class P:

    def __init__(self, nc, sems, eng, handles):
        self.nc = nc
        self.sems = sems
        self.eng = eng
        self.h = handles
        self.c = {s: 0 for s in sems}

    def do(self, eng, fn, incs=()):
        if self.eng == eng:
            inst = fn(self.h[eng])
            for s, n in incs:
                inst.then_inc(self.sems[s], n)
        for s, n in incs:
            self.c[s] += n

    def wait(self, eng, sem, val):
        assert val <= self.c[sem], \
            f"unsatisfiable wait {eng} {sem}>={val} (planned {self.c[sem]})"
        if val > 0 and self.eng == eng:
            self.h[eng].wait_ge(self.sems[sem], val)

    def barrier(self):
        snap = {e: self.c[e] for e in ("pe", "act", "dve")}
        for e in ("pe", "act", "dve", "sp"):
            for o in ("pe", "act", "dve"):
                if o != e:
                    self.wait(e, o, snap[o])


def srow(T, r, n=512):
    return T["stats"][r:r + 1, 0:n]


def build_program():
    nc = bass.Bass("TRN2", target_bir_lowering=False)

    dr = {}
    dr["xkvA"] = nc.dram_tensor("xkvA", [D, PA], F32, kind="ExternalInput").ap()
    dr["xkvB"] = nc.dram_tensor("xkvB", [D, PB], F32, kind="ExternalInput").ap()
    dr["validA"] = nc.dram_tensor("validA", [128, TA], F32, kind="ExternalInput").ap()
    dr["validB"] = nc.dram_tensor("validB", [128, TB], F32, kind="ExternalInput").ap()
    dr["tri"] = nc.dram_tensor("tri", [128, 128], F32, kind="ExternalInput").ap()
    dr["wq"] = nc.dram_tensor("wq", [HEADS, D, 128], F32, kind="ExternalInput").ap()
    dr["wk"] = nc.dram_tensor("wk", [NGRP, D, 128], F32, kind="ExternalInput").ap()
    dr["wv"] = nc.dram_tensor("wv", [D, 512], F32, kind="ExternalInput").ap()
    dr["wo"] = nc.dram_tensor("wo", [NCH, D, 128], F32, kind="ExternalInput").ap()
    dr["bo"] = nc.dram_tensor("bo", [128, NCH], F32, kind="ExternalInput").ap()
    dr["wfc"] = nc.dram_tensor("wfc", [NHH, D, 128], F32, kind="ExternalInput").ap()
    dr["wpr"] = nc.dram_tensor("wpr", [NCH, HFF, 128], F32, kind="ExternalInput").ap()
    dr["y2"] = nc.dram_tensor("y2", [D, 2 * CH], F32, kind="ExternalOutput").ap()

    from contextlib import ExitStack
    with ExitStack() as _es:
        def A(cm):
            return _es.enter_context(cm)
        aKY = A(nc.sbuf_tensor("aKY", [128, 8192], F32))
        aV = A(nc.sbuf_tensor("aV", [128, 8192], F32))
        aXQ = A(nc.sbuf_tensor("aXQ", [128, 8192], F32))
        aCTX = A(nc.sbuf_tensor("aCTX", [128, 8192], F32))
        aSTR = A(nc.sbuf_tensor("aSTR", [128, 12288], F32))
        tri_sb = A(nc.sbuf_tensor("tri_sb", [128, 128], F32))
        valid_sb = A(nc.sbuf_tensor("valid_sb", [128, TA + TB], F32))
        bo_sb = A(nc.sbuf_tensor("bo_sb", [128, NCH], F32))
        stats_sb = A(nc.sbuf_tensor("stats_sb", [128, 512], F32))
        scr = A(nc.sbuf_tensor("scr", [128, 512], F32))
        rsb = A(nc.sbuf_tensor("rsb", [128, 512], F32))
        mb = A(nc.sbuf_tensor("mb", [128, 64], F32))
        ones1 = A(nc.sbuf_tensor("ones1", [1, 128], F32))
        rowA = A(nc.sbuf_tensor("rowA", [1, 512], F32))
        rowB = A(nc.sbuf_tensor("rowB", [1, 512], F32))
        rowC = A(nc.sbuf_tensor("rowC", [1, 512], F32))
        rowD = A(nc.sbuf_tensor("rowD", [1, 64], F32))
        rowE = A(nc.sbuf_tensor("rowE", [1, 512], F32))
        ones16 = A(nc.sbuf_tensor("ones16", [128, 1], F16))
        ctmp = A(nc.sbuf_tensor("ctmp", [128, 512], F32))
        validsc = A(nc.sbuf_tensor("validsc", [128, TA + TB], F32))
        psA = A(nc.psum_tensor("psA", [128, 512], F32))
        psB = A(nc.psum_tensor("psB", [128, 512], F32))
        psC = A(nc.psum_tensor("psC", [128, 512], F32))
        psD = A(nc.psum_tensor("psD", [128, 512], F32))
        psE = A(nc.psum_tensor("psE", [128, 512], F32))
        psF = A(nc.psum_tensor("psF", [128, 512], F32))
        psS1 = A(nc.psum_tensor("psS1", [1, 512], F32))
        psS2 = A(nc.psum_tensor("psS2", [1, 512], F32))
        s_ld = A(nc.semaphore("ld"))
        s_gd = A(nc.semaphore("gd"))
        s_so = A(nc.semaphore("so"))
        s_pe = A(nc.semaphore("pe"))
        s_act = A(nc.semaphore("act"))
        s_dve = A(nc.semaphore("dve"))
        s_q = [A(nc.semaphore(f"q{k}")) for k in range(4)]
        s_r = [A(nc.semaphore(f"r{k}")) for k in range(4)]
        block = A(nc.Block())
        sems = {"ld": s_ld, "gd": s_gd, "so": s_so,
                "pe": s_pe, "act": s_act, "dve": s_dve}
        for k in range(4):
            sems[f"q{k}"] = s_q[k]
            sems[f"r{k}"] = s_r[k]
        T = {
            "aKY": aKY, "aV": aV, "aXQ": aXQ, "aCTX": aCTX, "aSTR": aSTR,
            "tri": tri_sb, "valid": valid_sb, "bo": bo_sb,
            "stats": stats_sb, "scr": scr, "rsb": rsb, "mb": mb, "ones1": ones1,
            "rowA": rowA, "rowB": rowB, "rowC": rowC, "rowD": rowD, "rowE": rowE,
            "ones16": ones16, "ctmp": ctmp, "validsc": validsc,
            "psA": psA, "psB": psB, "psC": psC, "psD": psD,
            "psE": psE, "psF": psF, "psS1": psS1, "psS2": psS2,
        }

        @block.tensor
        def _(tensor):
            program(P(nc, sems, "pe", {"pe": tensor}), T, dr)

        @block.scalar
        def _(scalar):
            program(P(nc, sems, "act", {"act": scalar}), T, dr)

        @block.vector
        def _(vector):
            program(P(nc, sems, "dve", {"dve": vector}), T, dr)

        @block.sync
        def _(sync):
            program(P(nc, sems, "sp", {"sp": sync}), T, dr)

        @block.gpsimd
        def _(gpsimd):
            program(P(nc, sems, "gp", {"gp": gpsimd}), T, dr)

    return nc



def program(P_, T, dr):
    st = {}
    prologue(P_, T, dr, st)
    stats_plan(P_, T, dr, st)
    macro(P_, T, dr, st, "A")
    macro(P_, T, dr, st, "B")
    P_.wait("sp", "so", P_.c["so"])


def prologue(P_, T, dr, st):
    P_.do("sp", lambda e: e.dma_start(out=T["tri"][:], in_=dr["tri"][:]), [("ld", 16)])
    P_.do("sp", lambda e: e.dma_start(out=T["valid"][:, 0:TA], in_=dr["validA"][:]), [("ld", 16)])
    P_.do("sp", lambda e: e.dma_start(out=T["valid"][:, TA:TA + TB], in_=dr["validB"][:]), [("ld", 16)])
    P_.do("sp", lambda e: e.dma_start(out=T["bo"][:], in_=dr["bo"][:]), [("ld", 16)])
    st["consts_ld"] = P_.c["ld"]
    P_.do("dve", lambda e: e.memset(T["ones1"][:], 1.0))
    P_.do("dve", lambda e: e.memset(T["mb"][:, MB_ONESC:MB_ONESC + 1], 1.0))
    P_.do("dve", lambda e: e.memset(T["rowD"][0:1, 48:49], EPS))
    P_.do("dve", lambda e: e.memset(T["ones16"][:], 1.0), [("dve", 1)])
    P_.wait("dve", "ld", st["consts_ld"])
    P_.do("dve", lambda e: e.tensor_scalar(T["validsc"][:], T["valid"][:], 16384.0,
                                           None, ALU.mult), [("dve", 1)])
    st["consts_dve"] = P_.c["dve"]
    P_.wait("pe", "dve", st["consts_dve"])
    P_.wait("act", "dve", st["consts_dve"])
    P_.wait("dve", "ld", st["consts_ld"])
    P_.wait("pe", "ld", st["consts_ld"])
    P_.wait("act", "ld", st["consts_ld"])


def stats_plan(P_, T, dr, st):
    ring = [T["aV"][:, 4096 + 512 * j: 4096 + 512 * (j + 1)] for j in range(4)]
    acc = T["aV"][:, 6400:6400 + N_STAT]

    def src_ap(j):
        if j < N_STAT_FC:
            a, q = divmod(j, 4)
            return dr["wfc"][a].rearrange("(q p) m -> p q m", p=128)[:, 4 * q:4 * (q + 1), :]
        j -= N_STAT_FC
        a, q = divmod(j, 10)
        return dr["wpr"][a].rearrange("(q p) m -> p q m", p=128)[:, 4 * q:4 * (q + 1), :]

    st["stats_dma_n"] = 0
    st["stats_abs_n"] = 0
    st["stats_absdone_act"] = []
    st["stats_ready_gd"] = []

    def emit_dma(j):
        if j >= 4:
            P_.wait("gp", "act", st["stats_absdone_act"][j - 4])
        sem = f"q{j % 4}"
        P_.do("gp", lambda e, j=j: e.dma_start(
            out=ring[j % 4].rearrange("p (q m) -> p q m", m=128),
            in_=src_ap(j)), [(sem, 16)])
        st["stats_ready_gd"].append((sem, P_.c[sem]))

    def pump(n):
        target = min(st["stats_abs_n"] + n, N_STAT)
        while st["stats_abs_n"] < target:
            while st["stats_dma_n"] < min(st["stats_abs_n"] + 4, N_STAT):
                emit_dma(st["stats_dma_n"])
                st["stats_dma_n"] += 1
            j = st["stats_abs_n"]
            sem_j, val_j = st["stats_ready_gd"][j]
            P_.wait("act", sem_j, val_j)
            P_.do("act", lambda e, j=j: e.activation(
                ring[j % 4][:], ring[j % 4][:], AF.Abs,
                accum_out=acc[:, j:j + 1]))
            P_.do("act", lambda e: e.activation(T["mb"][0:1, 44:45],
                                                T["mb"][0:1, 44:45], AF.Copy),
                  [("act", 1)])
            st["stats_absdone_act"].append(P_.c["act"])
            st["stats_abs_n"] = j + 1

    st["stats_pump"] = pump
    st["stats_acc"] = acc


def stats_finalize(P_, T, dr, st):
    if st.get("stats_final_done"):
        return
    st["stats_pump"](N_STAT)
    acc = st["stats_acc"]
    P_.wait("dve", "act", P_.c["act"])
    P_.do("dve", lambda e: e.reduce_sum(T["mb"][:, MB_RED0:MB_RED0 + 1],
                                        acc[:, 0:N_STAT_FC], axis=AX.X))
    P_.do("dve", lambda e: e.reduce_sum(T["mb"][:, MB_RED0 + 1:MB_RED0 + 2],
                                        acc[:, N_STAT_FC:N_STAT], axis=AX.X), [("dve", 1)])
    P_.wait("pe", "dve", P_.c["dve"])
    P_.do("pe", lambda e: e.matmul(T["psS1"][0:1, 0:1], T["mb"][:, MB_RED0:MB_RED0 + 1],
                                   T["mb"][:, MB_ONESC:MB_ONESC + 1], start=True, stop=True))
    P_.do("pe", lambda e: e.matmul(T["psS2"][0:1, 0:1], T["mb"][:, MB_RED0 + 1:MB_RED0 + 2],
                                   T["mb"][:, MB_ONESC:MB_ONESC + 1], start=True, stop=True),
          [("pe", 1)])
    P_.wait("dve", "pe", P_.c["pe"])
    r = T["rowD"][0:1, 0:16]
    NFC = float(NHH * D * 128)
    NPR = float(NCH * HFF * 128)
    P_.do("dve", lambda e: e.tensor_scalar(r[0:1, 0:1], T["psS1"][0:1, 0:1], 1.0 / NFC, 1e-5,
                                           ALU.mult, ALU.max))
    P_.do("dve", lambda e: e.tensor_scalar(r[0:1, 1:2], T["psS2"][0:1, 0:1], 1.0 / NPR, 1e-5,
                                           ALU.mult, ALU.max))
    P_.do("dve", lambda e: e.tensor_scalar(r[0:1, 2:3], r[0:1, 0:1], 0.7, None, ALU.mult))
    P_.do("dve", lambda e: e.tensor_scalar(r[0:1, 3:4], r[0:1, 1:2], 0.7, None, ALU.mult))
    P_.do("dve", lambda e: e.tensor_scalar(r[0:1, 4:5], r[0:1, 2:3], -1.0, None, ALU.mult))
    P_.do("dve", lambda e: e.tensor_scalar(r[0:1, 8:9], r[0:1, 0:1], 16.0, None, ALU.mult))
    P_.do("dve", lambda e: e.tensor_scalar(r[0:1, 5:6], r[0:1, 3:4], -1.0, None, ALU.mult))
    P_.do("dve", lambda e: e.tensor_scalar(r[0:1, 7:8], r[0:1, 1:2], r[0:1, 8:9],
                                           r[0:1, 8:9], ALU.mult, ALU.mult), [("dve", 1)])
    P_.wait("pe", "dve", P_.c["dve"])
    pairs = [(2, MB_THRF), (4, MB_NTHRF), (3, MB_THRP), (5, MB_NTHRP), (7, MB_CC)]
    for k, (src, dstc) in enumerate(pairs):
        P_.do("pe", lambda e, src=src, dstc=dstc: e.matmul(
            T["psE"][:, dstc:dstc + 1], T["ones1"][:], r[0:1, src:src + 1],
            start=True, stop=True), [("pe", 1)] if k == len(pairs) - 1 else ())
    P_.wait("dve", "pe", P_.c["pe"])
    P_.do("dve", lambda e: e.tensor_copy(T["mb"][:, MB_THRF:MB_CC + 1],
                                         T["psE"][:, MB_THRF:MB_CC + 1]), [("dve", 1)])
    st["ffn_consts_dve"] = P_.c["dve"]
    st["stats_final_done"] = True



def macro(P_, T, dr, st, which):
    TT = TA if which == "A" else TB
    xkv = dr["xkvA"] if which == "A" else dr["xkvB"]
    voff = 0 if which == "A" else TA
    out_col = 0 if which == "A" else CH

    for gp_i in (0, 1):
        kv_pass(P_, T, dr, st, which, gp_i, TT, xkv)
        if gp_i == 0:
            split_xq(P_, T, dr, st, which)
        attn(P_, T, dr, st, which, gp_i, TT, voff)
    wo_ln1(P_, T, dr, st, which)
    stats_finalize(P_, T, dr, st)
    ffn(P_, T, dr, st, which)
    ln2_out(P_, T, dr, st, which, out_col)


def k16_tile(T, gl, t128):
    t256, sub = divmod(t128, 2)
    base = 8192 * gl + 512 * t256
    k16 = T["aKY"][:].bitcast(F16)
    hi = k16[:, base + 128 * sub: base + 128 * sub + 128]
    lo = k16[:, base + 256 + 128 * sub: base + 256 + 128 * sub + 128]
    return hi, lo


def v16_tile(T, t128):
    v16 = T["aV"][:].bitcast(F16)
    base = 512 * t128
    return v16[:, base:base + 256], v16[:, base + 256:base + 512]


def xq16(T):
    x16 = T["aXQ"][:].bitcast(F16).rearrange("p (c n) -> p c n", n=1024)
    return x16[:, :, 0:512], x16[:, :, 512:1024]


def ctx16(T):
    c16 = T["aCTX"][:].bitcast(F16).rearrange("p (c n) -> p c n", n=1024)
    return c16[:, :, 0:512], c16[:, :, 512:1024]


def split_pair(P_, T, st, src32, hi, lo, tag):
    P_.do("act", lambda e: e.activation(hi, src32, AF.Copy), [("act", 1)])
    st[("split_hi", tag)] = P_.c["act"]
    P_.wait("dve", "act", P_.c["act"])
    P_.do("dve", lambda e: e.scalar_tensor_tensor(lo, src32, 1.0, hi,
                                                  ALU.mult, ALU.subtract), [("dve", 1)])
    st[("split_lo", tag)] = P_.c["dve"]


def kv_pass(P_, T, dr, st, which, gp_i, TT, xkv):
    P_.barrier()
    T256 = TT // 2
    voff = 0 if which == "A" else TA
    str16 = T["aSTR"][:].bitcast(F16)
    wkPh = str16[:, 0:8192].rearrange("p (c n) -> p c n", n=512)[:, :, 0:256]
    wkPl = str16[:, 0:8192].rearrange("p (c n) -> p c n", n=512)[:, :, 256:512]
    wvPh = str16[:, 8192:16384].rearrange("p (c n) -> p c n", n=512)[:, :, 0:256]
    wvPl = str16[:, 8192:16384].rearrange("p (c n) -> p c n", n=512)[:, :, 256:512]
    xPh = str16[:, 16384:24576].rearrange("p (c n) -> p c n", n=512)[:, :, 0:256]
    xPl = str16[:, 16384:24576].rearrange("p (c n) -> p c n", n=512)[:, :, 256:512]
    wdst = T["aCTX"][:, 4096:8192]
    wdst3 = wdst.rearrange("p (c n) -> p c n", n=256)

    for gl in range(2):
        g = 2 * gp_i + gl
        P_.do("sp", lambda e, g=g, gl=gl: e.dma_start(
            out=wdst3[:, :, 128 * gl:128 * (gl + 1)],
            in_=dr["wk"][g].rearrange("(c p) m -> p c m", p=128)), [("ld", 16)])
    P_.wait("act", "ld", P_.c["ld"])
    split_pair(P_, T, st, wdst3, wkPh, wkPl, ("wk", which, gp_i))
    P_.wait("sp", "act", P_.c["act"])
    P_.wait("sp", "dve", P_.c["dve"])
    P_.do("sp", lambda e, gp_i=gp_i: e.dma_start(
        out=wdst3,
        in_=dr["wv"][:, 256 * gp_i:256 * (gp_i + 1)].rearrange("(c p) m -> p c m", p=128)),
        [("ld", 16)])
    P_.wait("act", "ld", P_.c["ld"])
    split_pair(P_, T, st, wdst3, wvPh, wvPl, ("wv", which, gp_i))

    P_.do("dve", lambda e: e.memset(T["mb"][0:1, MB_KMAX:MB_KMAX + 2], 0.0), [("dve", 1)])

    KB = [T["psA"], T["psB"], T["psE"], T["psF"]]

    def kv_evacs(t):
        if t < 0:
            return
        sq = [T["scr"], T["rsb"]][t % 2]
        k16 = T["aKY"][:].bitcast(F16)
        for gl in range(2):
            idx = 2 * t + gl
            bank = KB[idx % 4]
            base = 8192 * gl + 512 * t
            khi = k16[:, base:base + 256]
            klo = k16[:, base + 256:base + 512]
            P_.wait("act", "pe", st[("kv_kmm", which, gp_i, idx)])
            P_.do("act", lambda e, bank=bank, khi=khi: e.activation(khi, bank[:, 0:256],
                                                                    AF.Copy))
            P_.do("act", lambda e, bank=bank, gl=gl, sq=sq: e.activation(
                sq[:, 256 * gl:256 * (gl + 1)], bank[:, 0:256], AF.Square), [("act", 1)])
            st[("kv_evac", which, gp_i, idx)] = P_.c["act"]
            P_.wait("dve", "act", P_.c["act"])
            P_.do("dve", lambda e, bank=bank, khi=khi, klo=klo: e.scalar_tensor_tensor(
                klo, bank[:, 0:256], 1.0, khi, ALU.mult, ALU.subtract), [("dve", 1)])
            st[("kv_evac_lo", which, gp_i, idx)] = P_.c["dve"]
        for sub in range(2):
            t128 = 2 * t + sub
            vbank = [T["psC"], T["psD"]][t128 % 2]
            vhi, vlo = v16_tile(T, t128)
            P_.wait("act", "pe", st[("kv_vmm", which, gp_i, t128)])
            P_.do("act", lambda e, vbank=vbank, vhi=vhi, t128=t128: e.activation(
                vhi, vbank[:, 0:256], AF.Copy,
                scale=T["valid"][:, voff + t128:voff + t128 + 1]), [("act", 1)])
            st[("v_evac", which, gp_i, t128)] = P_.c["act"]
            P_.wait("dve", "act", P_.c["act"])
            P_.do("dve", lambda e, vbank=vbank, vhi=vhi, vlo=vlo, t128=t128:
                  e.scalar_tensor_tensor(vlo, vbank[:, 0:256],
                                         T["valid"][:, voff + t128:voff + t128 + 1],
                                         vhi, ALU.mult, ALU.subtract), [("dve", 1)])
            st[("v_evac_lo", which, gp_i, t128)] = P_.c["dve"]

    def kv_knorms(t):
        if t < 0:
            return
        sq = [T["scr"], T["rsb"]][t % 2]
        for gl in range(2):
            idx = 2 * t + gl
            sbank = [T["psS1"], T["psS2"]][gl]
            P_.wait("pe", "act", st[("kv_evac", which, gp_i, idx)])
            P_.wait("pe", "dve", st.get(("knorm_dve", which, gp_i, idx - 2), 0))
            P_.do("pe", lambda e, gl=gl, sbank=sbank, sq=sq: e.matmul(
                sbank[0:1, 0:256], T["mb"][:, MB_ONESC:MB_ONESC + 1],
                sq[:, 256 * gl:256 * (gl + 1)], start=True, stop=True), [("pe", 1)])
            P_.wait("dve", "pe", P_.c["pe"])
            P_.do("dve", lambda e, sbank=sbank: e.reduce_max(
                srow(T, SR_KN, 1), sbank[0:1, 0:256], axis=AX.X))
            P_.do("dve", lambda e: e.tensor_copy(T["mb"][0:1, 36:40], T["mb"][0:1, 40:44]))
            P_.do("dve", lambda e, gl=gl: e.tensor_tensor(
                T["mb"][0:1, MB_KMAX + gl:MB_KMAX + gl + 1],
                T["mb"][0:1, MB_KMAX + gl:MB_KMAX + gl + 1],
                srow(T, SR_KN, 1), ALU.max), [("dve", 1)])
            st[("knorm_dve", which, gp_i, idx)] = P_.c["dve"]

    for t in range(T256):
        P_.wait("sp", "act", st.get(("split_hi", ("x", which, gp_i, t - 1)), 0))
        P_.wait("sp", "dve", st.get(("split_lo", ("x", which, gp_i, t - 1)), 0))
        if t == 0:
            P_.wait("sp", "act", st[("split_hi", ("wv", which, gp_i))])
            P_.wait("sp", "dve", st[("split_lo", ("wv", which, gp_i))])
        P_.do("sp", lambda e, t=t: e.dma_start(
            out=wdst3,
            in_=xkv[:, 256 * t:256 * (t + 1)].rearrange("(c p) n -> p c n", p=128)),
            [("ld", 16)])
        x_ld = P_.c["ld"]
        if t >= T256 - 2 and gp_i == 0:
            dcol = 256 * (t - (T256 - 2))
            P_.do("sp", lambda e, t=t, dcol=dcol: e.dma_start(
                out=T["aXQ"].rearrange("p (c n) -> p c n", n=512)[:, :, dcol:dcol + 256],
                in_=xkv[:, 256 * t:256 * (t + 1)].rearrange("(c p) n -> p c n", p=128)),
                [("ld", 16)])
            x_ld = P_.c["ld"]
        P_.wait("act", "ld", x_ld)
        P_.wait("act", "pe", st.get(("kv_mm_pe", which, gp_i, t - 1), 0))
        P_.wait("dve", "pe", st.get(("kv_mm_pe", which, gp_i, t - 1), 0))
        split_pair(P_, T, st, wdst3, xPh, xPl, ("x", which, gp_i, t))
        xs_hi = st[("split_hi", ("x", which, gp_i, t))]
        xs_lo = st[("split_lo", ("x", which, gp_i, t))]

        kv_evacs(t - 1)

        for gl in range(2):
            idx = 2 * t + gl
            bank = KB[idx % 4]
            P_.wait("pe", "act", max(st.get(("kv_evac", which, gp_i, idx - 4), 0), xs_hi))
            P_.wait("pe", "dve", st.get(("kv_evac_lo", which, gp_i, idx - 4), 0))
            passes = [(wkPh, xPh), (wkPh, xPl), (wkPl, xPh)]
            for pi, (wv_, xv_) in enumerate(passes):
                if pi == 1:
                    P_.wait("pe", "dve", xs_lo)
                for c in range(NCH):
                    last = (pi == 2 and c == NCH - 1)
                    P_.do("pe", lambda e, c=c, gl=gl, bank=bank, wv_=wv_, xv_=xv_,
                          pi=pi, last=last: e.matmul(
                        bank[:, 0:256], wv_[:, c, 128 * gl:128 * (gl + 1)], xv_[:, c, :],
                        start=(pi == 0 and c == 0), stop=last),
                        [("pe", 1)] if last else ())
            st[("kv_kmm", which, gp_i, idx)] = P_.c["pe"]

        for sub in range(2):
            t128 = 2 * t + sub
            vbank = [T["psC"], T["psD"]][t128 % 2]
            P_.wait("pe", "act", st.get(("v_evac", which, gp_i, t128 - 2), 0))
            P_.wait("pe", "dve", st.get(("v_evac_lo", which, gp_i, t128 - 2), 0))
            passes = [(xPh, wvPh), (xPl, wvPh), (xPh, wvPl)]
            for pi, (xv_, wv_) in enumerate(passes):
                for c in range(NCH):
                    last = (pi == 2 and c == NCH - 1)
                    P_.do("pe", lambda e, c=c, sub=sub, vbank=vbank, xv_=xv_, wv_=wv_,
                          pi=pi, last=last: e.matmul(
                              vbank[:, 0:256],
                              xv_[:, c, 128 * sub:128 * (sub + 1)], wv_[:, c, :],
                              start=(pi == 0 and c == 0), stop=last),
                        [("pe", 1)] if last else ())
            st[("kv_vmm", which, gp_i, t128)] = P_.c["pe"]
        st[("kv_mm_pe", which, gp_i, t)] = P_.c["pe"]

        kv_knorms(t - 1)

        if which == "A":
            st["stats_pump"](15)

    kv_evacs(T256 - 1)
    kv_knorms(T256 - 1)
    st[("kmax_dve", which, gp_i)] = P_.c["dve"]


def split_xq(P_, T, dr, st, which):
    P_.barrier()
    xqv = T["aXQ"].rearrange("p (c n) -> p c n", n=512)
    xh, xl = xq16(T)
    for c in range(NCH):
        P_.do("dve", lambda e, c=c: e.tensor_copy(T["ctmp"][:], xqv[:, c, :]), [("dve", 1)])
        P_.wait("act", "dve", P_.c["dve"])
        P_.do("act", lambda e, c=c: e.activation(xh[:, c, :], T["ctmp"][:], AF.Copy),
              [("act", 1)])
        P_.wait("dve", "act", P_.c["act"])
        P_.do("dve", lambda e, c=c: e.scalar_tensor_tensor(
            xl[:, c, :], T["ctmp"][:], 1.0, xh[:, c, :], ALU.mult, ALU.subtract),
            [("dve", 1)])
    st[("xqsplit_dve", which)] = P_.c["dve"]
    st[("xqsplit_act", which)] = P_.c["act"]


def attn(P_, T, dr, st, which, gp_i, TT, voff):
    P_.barrier()
    str16 = T["aSTR"][:].bitcast(F16)
    wqD = T["aSTR"][:, 0:2048]
    wqD3 = wqD.rearrange("p (c n) -> p c n", n=128)
    wqP_sl = []
    for k in range(2):
        base = T["aSTR"][:, 2048 * (k + 1):2048 * (k + 2)].bitcast(F16)
        v = base.rearrange("p (c n) -> p c n", n=256)
        wqP_sl.append((v[:, :, 0:128], v[:, :, 128:256]))
    qh_sl = [(str16[:, 12288:12800], str16[:, 12800:13312]),
             (str16[:, 13312:13824], str16[:, 13824:14336])]
    pt32 = [T["aSTR"][:, 7168 + 512 * k:7680 + 512 * k] for k in range(4)]
    ptP = [(str16[:, 18432 + 1024 * k:18944 + 1024 * k],
            str16[:, 18944 + 1024 * k:19456 + 1024 * k]) for k in range(4)]
    xh, xl = xq16(T)
    ch_, cl_ = ctx16(T)

    def wq_fetch(j):
        if j >= 8:
            return
        H = 8 * gp_i + j
        P_.wait("sp", "act", st.get(("split_hi", ("wq", which, gp_i, j - 1)), 0))
        P_.wait("sp", "dve", st.get(("split_lo", ("wq", which, gp_i, j - 1)), 0))
        P_.do("sp", lambda e, H=H: e.dma_start(
            out=wqD3, in_=dr["wq"][H].rearrange("(c p) m -> p c m", p=128)), [("ld", 16)])
        P_.wait("act", "ld", P_.c["ld"])
        P_.wait("act", "pe", st.get(("qproj_pe", which, H - 1), 0))
        P_.wait("dve", "pe", st.get(("qproj_pe", which, H - 1), 0))
        split_pair(P_, T, st, wqD3, wqP_sl[j % 2][0], wqP_sl[j % 2][1],
                   ("wq", which, gp_i, j))

    wq_fetch(0)
    for j in range(8):
        H = 8 * gp_i + j
        gl = (H // 4) - 2 * gp_i
        hb = j % 2
        qbank = [T["psE"], T["psF"]][hb]
        ctxps = [T["psC"], T["psD"]][hb]
        sps = [T["psS1"], T["psS2"]][hb]
        wqPh, wqPl = wqP_sl[j % 2]

        P_.wait("pe", "act", st[("split_hi", ("wq", which, gp_i, j))])
        P_.wait("pe", "dve", st[("split_lo", ("wq", which, gp_i, j))])
        if j >= 2:
            P_.wait("pe", "act", st[("q_evac_act", which, H - 2)])
            P_.wait("pe", "dve", st[("ctx_dve", which, H - 2)])
        for pi, (wv_, xv_) in enumerate([(wqPh, xh), (wqPh, xl), (wqPl, xh)]):
            for c in range(NCH):
                last = (pi == 2 and c == NCH - 1)
                P_.do("pe", lambda e, c=c, qbank=qbank, wv_=wv_, xv_=xv_, pi=pi,
                      last=last: e.matmul(
                    qbank[:], wv_[:, c, :], xv_[:, c, :],
                    start=(pi == 0 and c == 0), stop=last), [("pe", 1)] if last else ())
        st[("qproj_pe", which, H)] = P_.c["pe"]

        qhh, qhl = qh_sl[hb]
        P_.wait("act", "pe", P_.c["pe"])
        P_.do("act", lambda e, qhh=qhh, qbank=qbank: e.activation(qhh, qbank[:], AF.Copy))
        P_.do("act", lambda e, qbank=qbank: e.activation(T["scr"][:], qbank[:], AF.Square),
              [("act", 1)])
        st[("q_evac_act", which, H)] = P_.c["act"]
        q_evac = P_.c["act"]
        P_.wait("dve", "act", P_.c["act"])
        P_.do("dve", lambda e, qhh=qhh, qhl=qhl, qbank=qbank: e.scalar_tensor_tensor(
            qhl, qbank[:], 1.0, qhh, ALU.mult, ALU.subtract), [("dve", 1)])
        st[("q_lo_dve", which, H)] = P_.c["dve"]
        q_lo = P_.c["dve"]
        P_.wait("pe", "act", q_evac)
        P_.wait("pe", "dve", q_lo)
        P_.do("pe", lambda e, qbank=qbank: e.matmul(
            qbank[0:1, 0:512], T["mb"][:, MB_ONESC:MB_ONESC + 1], T["scr"][:],
            start=True, stop=True), [("pe", 1)])
        P_.wait("dve", "pe", P_.c["pe"])
        P_.do("dve", lambda e, qbank=qbank: e.reduce_max(srow(T, SR_KN, 1),
                                                         qbank[0:1, 0:512], axis=AX.X))
        P_.do("dve", lambda e: e.tensor_copy(T["mb"][0:1, 36:40], T["mb"][0:1, 40:44]))
        P_.do("dve", lambda e, gl=gl: e.tensor_tensor(
            T["rowD"][0:1, 34:35], srow(T, SR_KN, 1),
            T["mb"][0:1, MB_KMAX + gl:MB_KMAX + gl + 1], ALU.mult), [("dve", 1)])
        P_.wait("act", "dve", P_.c["dve"])
        P_.do("act", lambda e: e.activation(T["rowD"][0:1, 32:33], T["rowD"][0:1, 34:35],
                                            AF.Sqrt, scale=1.0 / 128.0), [("act", 1)])
        P_.wait("dve", "act", P_.c["act"])
        P_.do("dve", lambda e: e.tensor_scalar(T["rowD"][0:1, 32:33], T["rowD"][0:1, 32:33],
                                               -1.0, None, ALU.mult), [("dve", 1)])
        P_.wait("pe", "dve", P_.c["dve"])
        P_.do("pe", lambda e, qbank=qbank: e.matmul(
            qbank[:, 0:1], T["ones1"][:], T["rowD"][0:1, 32:33],
            start=True, stop=True), [("pe", 1)])
        P_.wait("dve", "pe", P_.c["pe"])
        P_.do("dve", lambda e, qbank=qbank: e.tensor_copy(T["mb"][:, MB_NEGM:MB_NEGM + 1],
                                                          qbank[:, 0:1]), [("dve", 1)])
        negm_dve = P_.c["dve"]

        wq_fetch(j + 1)

        stps = [T["psA"], T["psB"]]

        def emit_S(t):
            diag_j = t - (TT - 4)
            qlo = 128 * diag_j if diag_j >= 0 else 0
            bank = stps[t % 2]
            P_.wait("pe", "act", st.get(("exp_act", which, H, t - 2), 0))
            khi, klo = k16_tile(T, gl, t)
            first = True
            for (kv_, qv_) in ((khi, qhh), (khi, qhl), (klo, qhh)):
                last = (kv_ is klo)
                P_.do("pe", lambda e, t=t, qlo=qlo, bank=bank, kv_=kv_, qv_=qv_,
                      first=first, last=last: e.matmul(
                          bank[:, qlo:512], kv_, qv_[:, qlo:512],
                          start=first, stop=last), [("pe", 1)] if last else ())
                first = False
            st[("st_pe", which, H, t)] = P_.c["pe"]

        def qlo_of(t):
            diag_j = t - (TT - 4)
            return 128 * diag_j if diag_j >= 0 else 0

        def emit_lo(t):
            qlo = qlo_of(t)
            w = 512 - qlo
            pth, ptl = ptP[t % 4]
            P_.wait("dve", "act", st[("pth_act", which, H, t)])
            P_.do("dve", lambda e, t=t, w=w, pth=pth, ptl=ptl, voff=voff:
                  e.scalar_tensor_tensor(ptl[:, 0:w], pt32[t % 4][:, 0:w],
                                         T["validsc"][:, voff + t:voff + t + 1],
                                         pth[:, 0:w], ALU.mult, ALU.subtract),
                  [("dve", 1)])
            st[("ptl_dve", which, H, t)] = P_.c["dve"]
            P_.wait("pe", "dve", P_.c["dve"])
            vhi, vlo = v16_tile(T, t)
            P_.do("pe", lambda e, t=t, qlo=qlo, w=w, sps=sps, ptl=ptl: e.matmul(
                sps[0:1, qlo:512], T["ones16"][:], ptl[:, 0:w],
                start=False, stop=(t == TT - 1)))
            P_.do("pe", lambda e, t=t, qlo=qlo, w=w, gl=gl, ctxps=ctxps, vhi=vhi,
                  ptl=ptl: e.matmul(
                      ctxps[:, qlo:512], vhi[:, 128 * gl:128 * (gl + 1)], ptl[:, 0:w],
                      start=False, stop=(t == TT - 1)), [("pe", 1)])
            st[("lo_pe", which, H, t)] = P_.c["pe"]

        P_.wait("pe", "act", q_evac)
        emit_S(0)
        for t in range(TT):
            if t + 1 < TT:
                emit_S(t + 1)
            diag_j = t - (TT - 4)
            qlo = qlo_of(t)
            w = 512 - qlo
            bank = stps[t % 2]
            if diag_j >= 0:
                P_.wait("dve", "pe", st[("st_pe", which, H, t)])
                P_.do("dve", lambda e, bank=bank, qlo=qlo: e.tensor_tensor(
                    bank[:, qlo:qlo + 128], bank[:, qlo:qlo + 128], T["tri"][:], ALU.add),
                    [("dve", 1)])
                st[("tri_dve", which, H, t)] = P_.c["dve"]
            P_.wait("act", "pe", st[("st_pe", which, H, t)])
            if diag_j >= 0:
                P_.wait("act", "dve", st[("tri_dve", which, H, t)])
            if t == 0:
                P_.wait("act", "dve", negm_dve)
            P_.wait("act", "dve", st.get(("ptl_dve", which, H, t - 4), 0))
            P_.do("act", lambda e, t=t, bank=bank, qlo=qlo, w=w: e.activation(
                pt32[t % 4][:, 0:w], bank[:, qlo:512], AF.Exp,
                bias=T["mb"][:, MB_NEGM:MB_NEGM + 1], scale=SCALE), [("act", 1)])
            st[("exp_act", which, H, t)] = P_.c["act"]
            P_.wait("act", "pe", st.get(("lo_pe", which, H, t - 4), 0))
            pth, ptl = ptP[t % 4]
            P_.do("act", lambda e, t=t, w=w, pth=pth, voff=voff: e.activation(
                pth[:, 0:w], pt32[t % 4][:, 0:w], AF.Copy,
                scale=T["validsc"][:, voff + t:voff + t + 1]), [("act", 1)])
            st[("pth_act", which, H, t)] = P_.c["act"]
            P_.wait("pe", "act", P_.c["act"])
            vhi, vlo = v16_tile(T, t)
            P_.do("pe", lambda e, t=t, qlo=qlo, w=w, sps=sps, pth=pth: e.matmul(
                sps[0:1, qlo:512], T["ones16"][:], pth[:, 0:w],
                start=(t == 0), stop=False))
            P_.do("pe", lambda e, t=t, qlo=qlo, w=w, gl=gl, ctxps=ctxps, vhi=vhi,
                  pth=pth: e.matmul(
                      ctxps[:, qlo:512], vhi[:, 128 * gl:128 * (gl + 1)], pth[:, 0:w],
                      start=(t == 0), stop=False))
            P_.do("pe", lambda e, t=t, qlo=qlo, w=w, gl=gl, ctxps=ctxps, vlo=vlo,
                  pth=pth: e.matmul(
                      ctxps[:, qlo:512], vlo[:, 128 * gl:128 * (gl + 1)], pth[:, 0:w],
                      start=False, stop=False), [("pe", 1)])
            st[("av_pe", which, H, t)] = P_.c["pe"]
            if t >= 2:
                emit_lo(t - 2)
        emit_lo(TT - 2)
        emit_lo(TT - 1)

        P_.wait("dve", "pe", P_.c["pe"])
        P_.do("dve", lambda e, sps=sps: e.tensor_copy(T["rowA"][:], sps[0:1, 0:512]),
              [("dve", 1)])
        P_.wait("pe", "dve", P_.c["dve"])
        P_.do("pe", lambda e, qbank=qbank: e.matmul(
            qbank[:], T["ones1"][:], T["rowA"][:], start=True, stop=True), [("pe", 1)])
        P_.wait("dve", "pe", P_.c["pe"])
        P_.do("dve", lambda e, qbank=qbank: e.tensor_copy(T["rsb"][:], qbank[:]))
        P_.do("dve", lambda e: e.reciprocal(T["rsb"][:], T["rsb"][:]))
        P_.do("dve", lambda e, H=H, ctxps=ctxps: e.tensor_tensor(
            T["ctmp"][:], ctxps[:], T["rsb"][:], ALU.mult), [("dve", 1)])
        P_.wait("act", "dve", P_.c["dve"])
        P_.do("act", lambda e, H=H, ch_=ch_: e.activation(ch_[:, H, :], T["ctmp"][:],
                                                          AF.Copy), [("act", 1)])
        P_.wait("dve", "act", P_.c["act"])
        P_.do("dve", lambda e, H=H, ch_=ch_, cl_=cl_: e.scalar_tensor_tensor(
            cl_[:, H, :], T["ctmp"][:], 1.0, ch_[:, H, :], ALU.mult, ALU.subtract),
            [("dve", 1)])
        st[("ctx_dve", which, H)] = P_.c["dve"]
        if which == "A":
            st["stats_pump"](5)


def wo_ln1(P_, T, dr, st, which):
    P_.barrier()
    str16 = T["aSTR"][:].bitcast(F16)
    woD = T["aSTR"][:, 0:2048]
    woD3 = woD.rearrange("p (c n) -> p c n", n=128)
    woPh = str16[:, 4096:8192].rearrange("p (c n) -> p c n", n=256)[:, :, 0:128]
    woPl = str16[:, 4096:8192].rearrange("p (c n) -> p c n", n=256)[:, :, 128:256]
    v32 = T["aKY"].rearrange("p (c n) -> p c n", n=512)
    ch_, cl_ = ctx16(T)
    xh, xl = xq16(T)

    for do in range(NCH):
        P_.wait("sp", "act", st.get(("split_hi", ("wo", which, do - 1)), 0))
        P_.wait("sp", "dve", st.get(("split_lo", ("wo", which, do - 1)), 0))
        P_.do("sp", lambda e, do=do: e.dma_start(
            out=woD3, in_=dr["wo"][do].rearrange("(c p) m -> p c m", p=128)), [("ld", 16)])
        P_.wait("act", "ld", P_.c["ld"])
        P_.wait("act", "pe", st.get(("wo_pe", which, do - 1), 0))
        P_.wait("dve", "pe", st.get(("wo_pe", which, do - 1), 0))
        split_pair(P_, T, st, woD3, woPh, woPl, ("wo", which, do))
        P_.wait("pe", "act", st[("split_hi", ("wo", which, do))])
        P_.wait("pe", "dve", st[("split_lo", ("wo", which, do))])
        bank = [T["psA"], T["psB"]][do % 2]
        P_.wait("pe", "dve", st.get(("wo_evac", which, do - 2), 0))
        for pi, (wv_, cv_) in enumerate([(woPh, ch_), (woPh, cl_), (woPl, ch_)]):
            for c in range(NCH):
                last = (pi == 2 and c == NCH - 1)
                P_.do("pe", lambda e, c=c, bank=bank, wv_=wv_, cv_=cv_, pi=pi,
                      last=last: e.matmul(
                    bank[:], wv_[:, c, :], cv_[:, c, :],
                    start=(pi == 0 and c == 0), stop=last), [("pe", 1)] if last else ())
        st[("wo_pe", which, do)] = P_.c["pe"]
        P_.wait("dve", "pe", P_.c["pe"])
        P_.do("dve", lambda e, do=do, bank=bank, v32=v32: e.scalar_tensor_tensor(
            v32[:, do, :], bank[:], T["bo"][:, do:do + 1], xh[:, do, :],
            ALU.add, ALU.add))
        P_.do("dve", lambda e, do=do, v32=v32: e.tensor_tensor(
            v32[:, do, :], v32[:, do, :], xl[:, do, :], ALU.add), [("dve", 1)])
        st[("wo_evac", which, do)] = P_.c["dve"]
        P_.wait("pe", "dve", P_.c["dve"])
        P_.do("pe", lambda e, do=do, v32=v32: e.matmul(
            T["psS1"][0:1, 0:512], T["mb"][:, MB_ONESC:MB_ONESC + 1], v32[:, do, :],
            start=(do == 0), stop=(do == NCH - 1)), [("pe", 1)])
        P_.wait("act", "pe", P_.c["pe"])
        P_.wait("act", "pe", st.get(("sq_mm_pe", which, do - 1), 0))
        P_.do("act", lambda e, do=do, v32=v32: e.activation(T["scr"][:], v32[:, do, :],
                                                            AF.Square), [("act", 1)])
        P_.wait("pe", "act", P_.c["act"])
        P_.do("pe", lambda e, do=do: e.matmul(
            T["psS2"][0:1, 0:512], T["mb"][:, MB_ONESC:MB_ONESC + 1], T["scr"][:],
            start=(do == 0), stop=(do == NCH - 1)), [("pe", 1)])
        st[("sq_mm_pe", which, do)] = P_.c["pe"]

    ky16 = T["aKY"][:].bitcast(F16).rearrange("p (c n) -> p c n", n=1024)
    y1h = ky16[:, :, 0:512]
    y1l = ky16[:, :, 512:1024]
    ln_norm_pair(P_, T, st, v32, (y1h, y1l), which + "ln1")


def ln_norm(P_, T, st, src_v, dst_v, tag, pair=None):
    P_.wait("dve", "pe", P_.c["pe"])
    P_.do("dve", lambda e: e.tensor_scalar(T["rowB"][:], T["psS1"][0:1, 0:512],
                                           1.0 / D, None, ALU.mult))
    P_.do("dve", lambda e: e.tensor_scalar(T["rowA"][:], T["psS2"][0:1, 0:512],
                                           1.0 / D, None, ALU.mult))
    P_.do("dve", lambda e: e.tensor_tensor(T["rowE"][:], T["rowB"][:],
                                           T["rowB"][:], ALU.mult))
    P_.do("dve", lambda e: e.tensor_tensor(T["rowA"][:], T["rowA"][:],
                                           T["rowE"][:], ALU.subtract), [("dve", 1)])
    P_.wait("act", "dve", P_.c["dve"])
    P_.do("act", lambda e: e.activation(T["rowA"][:], T["rowA"][:], AF.Sqrt,
                                        bias=T["rowD"][0:1, 48:49]), [("act", 1)])
    P_.wait("dve", "act", P_.c["act"])
    P_.do("dve", lambda e: e.reciprocal(T["rowC"][:], T["rowA"][:]), [("dve", 1)])
    P_.wait("pe", "dve", P_.c["dve"])
    P_.do("pe", lambda e: e.matmul(T["psE"][:], T["ones1"][:], T["rowB"][:],
                                   start=True, stop=True))
    P_.do("pe", lambda e: e.matmul(T["psF"][:], T["ones1"][:], T["rowC"][:],
                                   start=True, stop=True), [("pe", 1)])
    P_.wait("dve", "pe", P_.c["pe"])
    for do in range(NCH):
        P_.do("dve", lambda e, do=do: e.tensor_tensor(
            dst_v[:, do, :], src_v[:, do, :], T["psE"][:], ALU.subtract))
        P_.do("dve", lambda e, do=do: e.tensor_tensor(
            dst_v[:, do, :], dst_v[:, do, :], T["psF"][:], ALU.mult), [("dve", 1)])
        if pair is not None:
            hi, lo = pair
            P_.do("dve", lambda e, do=do, hi=hi: e.tensor_copy(hi[:, do, :],
                                                               dst_v[:, do, :]))
            P_.do("dve", lambda e, do=do, hi=hi, lo=lo: e.scalar_tensor_tensor(
                lo[:, do, :], dst_v[:, do, :], 1.0, hi[:, do, :],
                ALU.mult, ALU.subtract), [("dve", 1)])
    st[("ln_done_dve", tag)] = P_.c["dve"]


def ln_norm_pair(P_, T, st, src_v, pair, tag):
    hi, lo = pair
    P_.wait("dve", "pe", P_.c["pe"])
    P_.do("dve", lambda e: e.tensor_scalar(T["rowB"][:], T["psS1"][0:1, 0:512],
                                           1.0 / D, None, ALU.mult))
    P_.do("dve", lambda e: e.tensor_scalar(T["rowA"][:], T["psS2"][0:1, 0:512],
                                           1.0 / D, None, ALU.mult))
    P_.do("dve", lambda e: e.tensor_tensor(T["rowE"][:], T["rowB"][:],
                                           T["rowB"][:], ALU.mult))
    P_.do("dve", lambda e: e.tensor_tensor(T["rowA"][:], T["rowA"][:],
                                           T["rowE"][:], ALU.subtract), [("dve", 1)])
    P_.wait("act", "dve", P_.c["dve"])
    P_.do("act", lambda e: e.activation(T["rowA"][:], T["rowA"][:], AF.Sqrt,
                                        bias=T["rowD"][0:1, 48:49]), [("act", 1)])
    P_.wait("dve", "act", P_.c["act"])
    P_.do("dve", lambda e: e.reciprocal(T["rowC"][:], T["rowA"][:]), [("dve", 1)])
    P_.wait("pe", "dve", P_.c["dve"])
    P_.do("pe", lambda e: e.matmul(T["psE"][:], T["ones1"][:], T["rowB"][:],
                                   start=True, stop=True))
    P_.do("pe", lambda e: e.matmul(T["psF"][:], T["ones1"][:], T["rowC"][:],
                                   start=True, stop=True), [("pe", 1)])
    P_.wait("dve", "pe", P_.c["pe"])
    tmp = T["rsb"]
    for do in range(NCH):
        P_.do("dve", lambda e, do=do, tmp=tmp: e.tensor_tensor(
            tmp[:], src_v[:, do, :], T["psE"][:], ALU.subtract))
        P_.do("dve", lambda e, do=do, tmp=tmp: e.tensor_tensor(
            tmp[:], tmp[:], T["psF"][:], ALU.mult))
        P_.do("dve", lambda e, do=do, hi=hi, tmp=tmp: e.tensor_copy(hi[:, do, :], tmp[:]))
        P_.do("dve", lambda e, do=do, hi=hi, lo=lo, tmp=tmp: e.scalar_tensor_tensor(
            lo[:, do, :], tmp[:], 1.0, hi[:, do, :], ALU.mult, ALU.subtract),
            [("dve", 1)])
    st[("ln_done_dve", tag)] = P_.c["dve"]


def ffn(P_, T, dr, st, which):
    P_.barrier()
    ky16 = T["aKY"][:].bitcast(F16).rearrange("p (c n) -> p c n", n=1024)
    y1h = ky16[:, :, 0:512]
    y1l = ky16[:, :, 512:1024]
    v2 = T["aXQ"].rearrange("p (c n) -> p c n", n=512)
    v16 = T["aV"][:].bitcast(F16).rearrange("p (c n) -> p c n", n=1024)
    h2h = v16[:, :, 0:512]
    h2l = v16[:, :, 512:1024]
    wfcb = [T["aSTR"][:, 2048 * k:2048 * (k + 1)] for k in range(4)]
    qfc_sl = [T["aSTR"][:, 8192 + 1024 * k:9216 + 1024 * k].bitcast(F16) for k in range(2)]
    wprb = [T["aCTX"][:, 1280 * k:1280 * (k + 1)] for k in range(4)]
    qpr_sl = [T["aCTX"][:, 5120 + 640 * k:5760 + 640 * k].bitcast(F16) for k in range(2)]
    scr2_sl = [T["aCTX"][:, 6400:6912], T["aCTX"][:, 6912:7424]]
    cdve = st["ffn_consts_dve"]

    def fc_dma(hh):
        if hh >= NHH:
            return
        sem = f"q{hh % 4}"
        P_.wait("sp", "dve", st.get(("tern_fc", which, hh - 4), 0))
        P_.do("sp", lambda e, hh=hh: e.dma_start(
            out=wfcb[hh % 4].rearrange("p (c n) -> p c n", n=128),
            in_=dr["wfc"][hh].rearrange("(c p) m -> p c m", p=128)), [(sem, 16)])
        st[("fc_ld", which, hh)] = (sem, P_.c[sem])

    def fc_tern(hh):
        if hh >= NHH:
            return
        qfc = qfc_sl[hh % 2]
        P_.wait("dve", *st[("fc_ld", which, hh)])
        P_.wait("dve", "pe", st.get(("mm1_pe", which, hh - 2), 0))
        P_.do("dve", lambda e, qfc=qfc, hh=hh: e.tensor_scalar(
            qfc[:], wfcb[hh % 4][:], T["mb"][:, MB_NTHRF:MB_NTHRF + 1], None, ALU.is_lt))
        P_.do("dve", lambda e, qfc=qfc, hh=hh: e.scalar_tensor_tensor(
            qfc[:], wfcb[hh % 4][:], T["mb"][:, MB_THRF:MB_THRF + 1], qfc[:],
            ALU.is_gt, ALU.subtract), [("dve", 1)])
        st[("tern_fc", which, hh)] = P_.c["dve"]

    def h2lo(hh):
        if hh < 0 or hh >= NHH:
            return
        hc = hh % HH_PER_Q
        sc2 = scr2_sl[hh % 2]
        P_.wait("dve", "act", st[("h_evac", which, hh)])
        P_.do("dve", lambda e, hc=hc, sc2=sc2: e.scalar_tensor_tensor(
            h2l[:, hc, :], sc2[:], 1.0, h2h[:, hc, :], ALU.mult, ALU.subtract),
            [("dve", 1)])
        st[("h2l_dve", which, hh)] = P_.c["dve"]

    def pr_dma(dd):
        if dd >= 16:
            return
        q0, do = divmod(dd, 16)
        sem = f"r{dd % 4}"
        P_.wait("sp", "dve", st.get(("tern_pr", which, qq * 16 + dd - 4), 0))
        P_.do("sp", lambda e, dd=dd: e.dma_start(
            out=wprb[dd % 4].rearrange("p (c n) -> p c n", n=128),
            in_=dr["wpr"][dd][1280 * qq:1280 * (qq + 1), :].rearrange(
                "(c p) m -> p c m", p=128)), [(sem, 16)])
        st[("pr_ld", which, qq, dd)] = (sem, P_.c[sem])

    def pr_tern(dd):
        if dd >= 16:
            return
        qpr = qpr_sl[dd % 2]
        P_.wait("dve", *st[("pr_ld", which, qq, dd)])
        if dd == 0:
            P_.wait("dve", "dve", st[("h2l_dve", which, qq * HH_PER_Q + HH_PER_Q - 1)])
            P_.wait("dve", "act", st[("h_evac", which, qq * HH_PER_Q + HH_PER_Q - 1)])
        P_.wait("dve", "pe", st.get(("mm2_pe", which, qq * 16 + dd - 2), 0))
        P_.do("dve", lambda e, qpr=qpr: e.tensor_scalar(
            qpr[:], wprb[dd % 4][:], T["mb"][:, MB_NTHRP:MB_NTHRP + 1], None, ALU.is_lt))
        P_.do("dve", lambda e, qpr=qpr: e.scalar_tensor_tensor(
            qpr[:], wprb[dd % 4][:], T["mb"][:, MB_THRP:MB_THRP + 1], qpr[:],
            ALU.is_gt, ALU.subtract), [("dve", 1)])
        st[("tern_pr", which, qq * 16 + dd)] = P_.c["dve"]

    def v2ev(dd):
        if dd < 0 or dd >= 16:
            return
        do = dd
        bank = [T["psC"], T["psD"]][dd % 2]
        P_.wait("dve", "pe", st[("mm2_pe", which, qq * 16 + dd)])
        if qq == 0:
            P_.do("dve", lambda e, do=do, bank=bank: e.scalar_tensor_tensor(
                v2[:, do, :], bank[:], T["mb"][:, MB_CC:MB_CC + 1], y1h[:, do, :],
                ALU.mult, ALU.add))
            P_.do("dve", lambda e, do=do: e.tensor_tensor(
                v2[:, do, :], v2[:, do, :], y1l[:, do, :], ALU.add), [("dve", 1)])
        else:
            P_.do("dve", lambda e, do=do, bank=bank: e.scalar_tensor_tensor(
                v2[:, do, :], bank[:], T["mb"][:, MB_CC:MB_CC + 1], v2[:, do, :],
                ALU.mult, ALU.add), [("dve", 1)])
        st[("v2_evac", which, qq * 16 + dd)] = P_.c["dve"]

    for qq in range(NQUARTER):
        h0 = qq * HH_PER_Q
        if qq == 0:
            fc_dma(h0)
            fc_dma(h0 + 1)
            fc_tern(h0)
        for hc in range(HH_PER_Q):
            hh = h0 + hc
            fc_dma(hh + 2)
            fc_tern(hh + 1)
            P_.wait("pe", "dve", st[("tern_fc", which, hh)])
            bank = [T["psA"], T["psB"]][hc % 2]
            P_.wait("pe", "act", st.get(("h_evac", which, hh - 2), 0))
            qfc = qfc_sl[hh % 2]
            qv = qfc.rearrange("p (c n) -> p c n", n=128)
            for half in range(2):
                yv = y1h if half == 0 else y1l
                for c in range(NCH):
                    P_.do("pe", lambda e, c=c, bank=bank, qv=qv, yv=yv: e.matmul(
                        bank[:], qv[:, c, :], yv[:, c, :],
                        start=(half == 0 and c == 0), stop=(half == 1 and c == NCH - 1)),
                        [("pe", 1)] if (half == 1 and c == NCH - 1) else ())
            st[("mm1_pe", which, hh)] = P_.c["pe"]
            sc2 = scr2_sl[hh % 2]
            P_.wait("act", "pe", P_.c["pe"])
            P_.wait("act", "dve", st.get(("h2l_dve", which, hh - 2), 0))
            P_.do("act", lambda e, bank=bank: e.activation(T["scr"][:], bank[:], AF.Relu))
            P_.do("act", lambda e, sc2=sc2: e.activation(sc2[:], T["scr"][:], AF.Square,
                                                         scale=0.0625))
            P_.do("act", lambda e, hc=hc, sc2=sc2: e.activation(h2h[:, hc, :], sc2[:],
                                                                AF.Copy), [("act", 1)])
            st[("h_evac", which, hh)] = P_.c["act"]
            h2lo(hh - 1)
        h2lo(h0 + HH_PER_Q - 1)

        pr_dma(0)
        pr_dma(1)
        pr_tern(0)
        for dd in range(16):
            pr_dma(dd + 2)
            pr_tern(dd + 1)
            P_.wait("pe", "dve", st[("tern_pr", which, qq * 16 + dd)])
            bank = [T["psC"], T["psD"]][dd % 2]
            P_.wait("pe", "dve", st.get(("v2_evac", which, qq * 16 + dd - 2), 0))
            qpr = qpr_sl[dd % 2]
            qprv = qpr.rearrange("p (c n) -> p c n", n=128)
            for half in range(2):
                hv = h2h if half == 0 else h2l
                for hc in range(HH_PER_Q):
                    P_.do("pe", lambda e, hc=hc, bank=bank, qprv=qprv, hv=hv: e.matmul(
                        bank[:], qprv[:, hc, :], hv[:, hc, :],
                        start=(half == 0 and hc == 0),
                        stop=(half == 1 and hc == HH_PER_Q - 1)),
                        [("pe", 1)] if (half == 1 and hc == HH_PER_Q - 1) else ())
            st[("mm2_pe", which, qq * 16 + dd)] = P_.c["pe"]
            v2ev(dd - 1)
        v2ev(15)


def ln2_out(P_, T, dr, st, which, out_col):
    P_.barrier()
    v2 = T["aXQ"].rearrange("p (c n) -> p c n", n=512)
    for do in range(NCH):
        P_.do("pe", lambda e, do=do, v2=v2: e.matmul(
            T["psS1"][0:1, 0:512], T["mb"][:, MB_ONESC:MB_ONESC + 1], v2[:, do, :],
            start=(do == 0), stop=(do == NCH - 1)), [("pe", 1)])
        P_.wait("act", "pe", P_.c["pe"])
        P_.wait("act", "pe", st.get(("sq2_mm_pe", which, do - 1), 0))
        P_.do("act", lambda e, do=do, v2=v2: e.activation(T["scr"][:], v2[:, do, :],
                                                          AF.Square), [("act", 1)])
        P_.wait("pe", "act", P_.c["act"])
        P_.do("pe", lambda e, do=do: e.matmul(
            T["psS2"][0:1, 0:512], T["mb"][:, MB_ONESC:MB_ONESC + 1], T["scr"][:],
            start=(do == 0), stop=(do == NCH - 1)), [("pe", 1)])
        st[("sq2_mm_pe", which, do)] = P_.c["pe"]

    ln_norm(P_, T, st, v2, v2, which + "ln2")
    P_.wait("act", "dve", P_.c["dve"])
    for do in range(NCH):
        P_.do("act", lambda e, do=do, v2=v2: e.dma_start(
            out=dr["y2"][128 * do:128 * (do + 1), out_col:out_col + CH],
            in_=v2[:, do, :]), [("so", 16)])
    P_.wait("sp", "so", P_.c["so"])



def kernel(L0_attn_x, L0_attn_Wq, L0_attn_Wk, L0_attn_Wv, L0_attn_Wo, L0_attn_bo,
           L0_ffn_w_fc, L0_ffn_w_proj):
    x = np.asarray(L0_attn_x, dtype=np.float32)
    Wq = np.asarray(L0_attn_Wq, dtype=np.float32)
    Wk = np.asarray(L0_attn_Wk, dtype=np.float32)
    Wv = np.asarray(L0_attn_Wv, dtype=np.float32)
    Wo = np.asarray(L0_attn_Wo, dtype=np.float32)
    bo = np.asarray(L0_attn_bo, dtype=np.float32)
    Wfc = np.asarray(L0_ffn_w_fc, dtype=np.float32)
    Wpr = np.asarray(L0_ffn_w_proj, dtype=np.float32)

    wq_blk = np.ascontiguousarray(Wq.reshape(D, HEADS, 128).transpose(1, 0, 2))
    wk_blk = np.ascontiguousarray(Wk.reshape(D, NGRP, 128).transpose(1, 0, 2))
    wo_blk = np.ascontiguousarray(Wo.reshape(D, NCH, 128).transpose(1, 0, 2))
    bo_t = np.ascontiguousarray(bo.reshape(NCH, 128).T)
    wfc_blk = np.ascontiguousarray(Wfc.T.reshape(D, NHH, 128).transpose(1, 0, 2))
    wpr_blk = np.ascontiguousarray(Wpr.T.reshape(HFF, NCH, 128).transpose(1, 0, 2))
    tri = np.where(np.arange(128)[:, None] <= np.arange(128)[None, :],
                   np.float32(0.0), np.float32(NEG_BIG)).astype(np.float32)

    in_maps = []
    for core in range(8):
        b, i = divmod(core, 4)
        xt = np.ascontiguousarray(x[b].T)
        cA, cB = i, 4 + i
        pA, pB = CH * cA, CH * cB

        xkvA = np.zeros((D, PA), dtype=np.float32)
        xkvA[:, :pA] = xt[:, :pA]
        xkvA[:, PA - CH:] = xt[:, CH * cA:CH * (cA + 1)]
        xkvB = np.zeros((D, PB), dtype=np.float32)
        xkvB[:, :pB] = xt[:, :pB]
        xkvB[:, PB - CH:] = xt[:, CH * cB:CH * (cB + 1)]

        def mk_valid(Pn, p_real):
            idx = np.arange(Pn)
            v = ((idx < p_real) | (idx >= Pn - CH)).astype(np.float32)
            return np.ascontiguousarray(v.reshape(Pn // 128, 128).T)

        in_maps.append({
            "xkvA": xkvA, "xkvB": xkvB,
            "validA": mk_valid(PA, pA), "validB": mk_valid(PB, pB),
            "tri": tri,
            "wq": wq_blk, "wk": wk_blk, "wv": Wv, "wo": wo_blk, "bo": bo_t,
            "wfc": wfc_blk, "wpr": wpr_blk,
        })

    nc = build_program()
    res = run_bass_kernel_spmd(nc, in_maps, core_ids=list(range(8)))

    yt = np.empty((2, D, SEQ), dtype=np.float32)
    for core in range(8):
        b, i = divmod(core, 4)
        o = res.results[core]["y2"]
        yt[b][:, CH * i:CH * (i + 1)] = o[:, 0:CH]
        yt[b][:, CH * (4 + i):CH * (5 + i)] = o[:, CH:2 * CH]
    return np.ascontiguousarray(yt.transpose(0, 2, 1))
